# version 08
# baseline (speedup 1.0000x reference)
import contextlib
import numpy as np

import concourse.bass as bass
import concourse.mybir as mybir
from concourse.bass_utils import run_bass_kernel_spmd

F32 = mybir.dt.float32
BF16 = mybir.dt.bfloat16
AF = mybir.ActivationFunctionType
ALU = mybir.AluOpType

H = 4096
S = 1024
FF = 9984
NG = 6
NTERM = 7
NHC = H // 128
NSC = S // 128
NFC = FF // 128
EPS = 1e-5
LOG_THD = float(np.log(np.float32(2.5)))


class Sem:
    def __init__(self, es, nc, name):
        self.h = es.enter_context(nc.semaphore(name))
        self.n = 0

    def tick(self, inst, k=1):
        inst.then_inc(self.h, k)
        self.n += k
        return self.n


class Guard:

    def __init__(self, nslots):
        self.p = [None] * nslots

    def wait(self, eng, i):
        c = self.p[i % len(self.p)]
        if c is not None:
            for sem, val in c:
                eng.wait_ge(sem.h, val)

    def set(self, i, sem, val=None):
        self.p[i % len(self.p)] = [(sem, sem.n if val is None else val)]

    def add(self, i, sem, val=None):
        self.p[i % len(self.p)].append((sem, sem.n if val is None else val))


class DmaRing:

    def __init__(self, es, nc, name, n):
        self.n = n
        self.s = [Sem(es, nc, f"{name}{i}") for i in range(n)]
        self.g = Guard(n)

    def slot(self, i):
        return i % self.n

    def sem(self, i):
        return self.s[i % self.n]

    def waitv(self, i):
        sm = self.s[i % self.n]
        return sm.h, sm.n


def build(tc=2048, dbg=False):
    assert tc % 512 == 0 and tc <= 2048
    nt = tc // 512

    nc = bass.Bass("TRN2", target_bir_lowering=False)

    dt_in = lambda name, shape: nc.dram_tensor(name, shape, F32, kind="ExternalInput").ap()
    xT = dt_in("xT", [H, tc])
    upT = dt_in("upT", [H, S])
    gateT = dt_in("gateT", [H, S])
    flT = dt_in("flT", [NTERM, S, H])
    ffngT = dt_in("ffngT", [H, FF])
    ffnuT = dt_in("ffnuT", [H, FF])
    downT = dt_in("downT", [FF, H])
    WT = dt_in("WT", [H, H])
    AT = dt_in("AT", [H, 128])
    BT = dt_in("BT", [128, H])
    lp_col = dt_in("lp_col", [128, NSC])
    bias_col = dt_in("bias_col", [128, NHC])
    disc_cols = dt_in("disc_cols", [128, NG])
    lw_col = dt_in("lw_col", [128, 1])

    outT = nc.dram_tensor("outT", [H, tc], F32, kind="ExternalOutput").ap()

    skind = "ExternalOutput" if dbg else "Internal"
    au_scr = nc.dram_tensor("au_scr", [S, tc], BF16, kind=skind).ap()
    aus_scr = nc.dram_tensor("aus_scr", [S, tc], BF16, kind=skind).ap()
    logd_scr = nc.dram_tensor("logd_scr", [S, tc], F32, kind=skind).ap()
    y_scr = nc.dram_tensor("y_scr", [FF, tc], BF16, kind=skind).ap()
    s1_scr = nc.dram_tensor("s1_scr", [H, tc], F32, kind=skind).ap()
    x1_scr = nc.dram_tensor("x1_scr", [H, tc], F32, kind=skind).ap()
    fl_bscr = nc.dram_tensor("fl_bscr", [8 * NTERM, 128, NSC, 512], BF16, kind="Internal").ap()
    dw_bscr = nc.dram_tensor("dw_bscr", [32, 128, 39, 256], BF16, kind="Internal").ap()

    xT_r = xT.rearrange("(hc p) t -> p hc t", p=128)
    upT_r = upT.rearrange("(hc p) s -> p hc s", p=128)
    gateT_r = gateT.rearrange("(hc p) s -> p hc s", p=128)
    flT_r = flT.rearrange("g (sc p) h -> g p sc h", p=128)
    ffngT_r = ffngT.rearrange("(hc p) f -> p hc f", p=128)
    ffnuT_r = ffnuT.rearrange("(hc p) f -> p hc f", p=128)
    downT_r = downT.rearrange("(fc p) h -> p fc h", p=128)
    WT_r = WT.rearrange("(dc p) o -> p dc o", p=128)
    AT_r = AT.rearrange("(dc p) r -> p dc r", p=128)
    au_r = au_scr.rearrange("(sc p) t -> p sc t", p=128)
    aus_r = aus_scr.rearrange("(sc p) t -> p sc t", p=128)
    logd_r = logd_scr.rearrange("(sc p) t -> p sc t", p=128)
    y_r = y_scr.rearrange("(fc p) t -> p fc t", p=128)
    x1_r = x1_scr.rearrange("(hc p) t -> p hc t", p=128)
    s1_r = s1_scr.rearrange("(hc p) t -> p hc t", p=128)

    es = contextlib.ExitStack()
    scopes = [es]

    def sb_tmp(name, shape, dt):
        return scopes[-1].enter_context(nc.sbuf_tensor(name, shape, dt))

    @contextlib.contextmanager
    def tmp_scope():
        s = contextlib.ExitStack()
        scopes.append(s)
        try:
            with s:
                yield
        finally:
            scopes.pop()

    with es:
        pe = Sem(es, nc, "pe")
        ve = Sem(es, nc, "ve")
        ac = Sem(es, nc, "ac")
        po = Sem(es, nc, "po")

        ones_bf = sb_tmp("ones_bf", [128, 1], BF16)
        ones_row = sb_tmp("ones_row", [1, 128], F32)
        lp_sb = sb_tmp("lp_sb", [128, NSC], F32)
        bias_sb = sb_tmp("bias_sb", [128, NHC], F32)
        disc_sb = sb_tmp("disc_sb", [128, NG], F32)
        negd = sb_tmp("negd", [128, NG], F32)
        lw_sb = sb_tmp("lw_sb", [128, 1], F32)
        coefcol = sb_tmp("coefcol", [128, 1], F32)

        ps = es.enter_context(nc.psum_tensor("ps", [128, 8, 512], F32))

        def emit_ln(blk, pfx, src_r, hidden_tile, pre_waits):
            st = sb_tmp(pfx + "lnst", [1, 4, 512], F32)
            rows2 = sb_tmp(pfx + "rows2", [1, 2, tc], F32)
            rstd_b = sb_tmp(pfx + "rstd_b", [128, tc], F32)
            mrstd_b = sb_tmp(pfx + "mrstd_b", [128, tc], F32)
            xbuf = sb_tmp(pfx + "lnx", [128, 2, tc], F32)
            xr = DmaRing(es, nc, pfx + "lnxr", 2)
            xqb = sb_tmp(pfx + "lnq", [128, 2, tc], BF16)
            xbb = sb_tmp(pfx + "lnb", [128, 2, tc], BF16)
            qg = Guard(2)
            bg = Guard(2)
            xcw = 512
            t1v = sb_tmp(pfx + "lnt1v", [128, xcw], F32)
            t1p = sb_tmp(pfx + "lnt1p", [128, xcw], F32)

            first = True
            for hc in range(NHC):
                sl = hc % 2
                sl2 = hc % 2

                @blk.sync
                def _(sy, hc=hc, sl=sl, first=first):
                    if first:
                        for hh, vv in pre_waits:
                            sy.wait_ge(hh, vv)
                    xr.g.wait(sy, hc)
                    xr.sem(hc).tick(sy.dma_start(xbuf[:, sl, :], src_r[:, hc, :]), 16)
                ldh, ldv = xr.waitv(hc)
                first = False

                @blk.scalar
                def _(sc, sl=sl, sl2=sl2, ldh=ldh, ldv=ldv, hc=hc):
                    sc.wait_ge(ldh, ldv)
                    qg.wait(sc, hc)
                    ac.tick(sc.activation(xqb[:, sl2, :], xbuf[:, sl, :], AF.Square))
                acv = ac.n

                @blk.vector
                def _(v, sl=sl, sl2=sl2, ldh=ldh, ldv=ldv, hc=hc):
                    v.wait_ge(ldh, ldv)
                    bg.wait(v, hc)
                    ve.tick(v.tensor_copy(xbb[:, sl2, :], xbuf[:, sl, :]))
                vev = ve.n

                @blk.tensor
                def _(t, sl2=sl2, hc=hc, acv=acv, vev=vev):
                    t.wait_ge(ac.h, acv)
                    t.wait_ge(ve.h, vev)
                    st = (hc == 0)
                    sp = (hc == NHC - 1)
                    inst = None
                    for j in range(nt):
                        tsl = slice(j * 512, (j + 1) * 512)
                        inst = t.matmul(ps[0:1, j, :], ones_bf[:, :], xbb[:, sl2, tsl], start=st, stop=sp)
                        inst = t.matmul(ps[0:1, 4 + j, :], ones_bf[:, :], xqb[:, sl2, tsl], start=st, stop=sp)
                    pe.tick(inst)
                xr.g.set(hc, pe)
                qg.set(hc, pe)
                bg.set(hc, pe)
            mm_done = pe.n

            for j in range(nt):
                jsl = slice(j * 512, (j + 1) * 512)

                @blk.vector
                def _(v, j=j, jsl=jsl, mm_done=mm_done):
                    if j == 0:
                        v.wait_ge(pe.h, mm_done)
                    v.drain()
                    v.tensor_copy(st[0:1, 0, :], ps[0:1, j, :])
                    v.tensor_copy(st[0:1, 1, :], ps[0:1, 4 + j, :])
                    v.drain()
                    v.tensor_scalar(st[0:1, 2, :], st[0:1, 0, :], 1.0 / H, None, ALU.mult)
                    v.tensor_scalar(st[0:1, 3, :], st[0:1, 1, :], 1.0 / H, None, ALU.mult)
                    v.drain()
                    v.tensor_tensor(st[0:1, 0, :], st[0:1, 2, :], st[0:1, 2, :], ALU.mult)
                    v.drain()
                    ve.tick(v.tensor_tensor(st[0:1, 1, :], st[0:1, 3, :], st[0:1, 0, :], ALU.subtract))
                vev = ve.n

                @blk.scalar
                def _(sc, j=j, jsl=jsl, vev=vev):
                    sc.wait_ge(ve.h, vev)
                    sc.activation(st[0:1, 3, :], st[0:1, 1, :], AF.Ln, bias=np.float32(EPS))
                    sc.drain()
                    ac.tick(sc.activation(rows2[0:1, 0, jsl], st[0:1, 3, :], AF.Exp, scale=-0.5))
                acv = ac.n

                @blk.vector
                def _(v, j=j, jsl=jsl, acv=acv):
                    v.wait_ge(ac.h, acv)
                    ve.tick(v.tensor_tensor(rows2[0:1, 1, jsl], st[0:1, 2, :], rows2[0:1, 0, jsl], ALU.mult))
            vev = ve.n

            @blk.tensor
            def _(t):
                t.wait_ge(ve.h, vev)
                inst = None
                for j in range(nt):
                    inst = t.matmul(ps[:, j, :], ones_row[:, :], rows2[0:1, 0, j * 512:(j + 1) * 512], start=True, stop=True)
                    inst = t.matmul(ps[:, 4 + j, :], ones_row[:, :], rows2[0:1, 1, j * 512:(j + 1) * 512], start=True, stop=True)
                pe.tick(inst)
            pev = pe.n

            @blk.vector
            def _(v):
                v.wait_ge(pe.h, pev)
                inst = None
                for j in range(nt):
                    inst = v.tensor_copy(rstd_b[:, j * 512:(j + 1) * 512], ps[:, j, :])
                    inst = v.tensor_copy(mrstd_b[:, j * 512:(j + 1) * 512], ps[:, 4 + j, :])
                ve.tick(inst)
            bc_v = ve.n

            for hc in range(NHC):
                i = NHC + hc
                sl = i % 2

                @blk.sync
                def _(sy, i=i, sl=sl, hc=hc):
                    xr.g.wait(sy, i)
                    xr.sem(i).tick(sy.dma_start(xbuf[:, sl, :], src_r[:, hc, :]), 16)
                ldh, ldv = xr.waitv(i)

                if hc % 2 == 0:
                    @blk.vector
                    def _(v, sl=sl, hc=hc, ldh=ldh, ldv=ldv):
                        v.wait_ge(ldh, ldv)
                        for q in range(tc // xcw):
                            qs = slice(q * xcw, (q + 1) * xcw)
                            v.drain()
                            v.tensor_tensor(t1v[:, :], xbuf[:, sl, qs], rstd_b[:, qs], ALU.mult)
                            v.drain()
                            ve.tick(v.tensor_tensor(hidden_tile[:, hc, qs], t1v[:, :], mrstd_b[:, qs], ALU.subtract))
                    xr.g.set(i, ve)
                else:
                    @blk.gpsimd
                    def _(g, sl=sl, hc=hc, ldh=ldh, ldv=ldv, bc_v=bc_v):
                        g.wait_ge(ldh, ldv)
                        g.wait_ge(ve.h, bc_v)
                        for q in range(tc // xcw):
                            qs = slice(q * xcw, (q + 1) * xcw)
                            g.drain()
                            g.tensor_tensor(t1p[:, :], xbuf[:, sl, qs], rstd_b[:, qs], ALU.mult)
                            g.drain()
                            po.tick(g.tensor_tensor(hidden_tile[:, hc, qs], t1p[:, :], mrstd_b[:, qs], ALU.subtract))
                    xr.g.set(i, po)

        with tmp_scope():
            hidden = sb_tmp("hidden", [128, NHC, tc], BF16)

            with tmp_scope():
                tabr = DmaRing(es, nc, "tab", 1)
                with nc.named_scope("p0"), nc.Block("p0") as blk:
                    @blk.sync
                    def _(sy):
                        tabr.sem(0).tick(sy.dma_start(lp_sb[:, :], lp_col[:, :]), 16)
                        tabr.sem(0).tick(sy.dma_start(bias_sb[:, :], bias_col[:, :]), 16)
                        tabr.sem(0).tick(sy.dma_start(disc_sb[:, :], disc_cols[:, :]), 16)
                        tabr.sem(0).tick(sy.dma_start(lw_sb[:, :], lw_col[:, :]), 16)
                    tabh, tabv = tabr.waitv(0)

                    @blk.vector
                    def _(v):
                        v.memset(ones_bf[:, :], 1.0)
                        v.wait_ge(tabh, tabv)
                        v.tensor_scalar(coefcol[:, :], lw_sb[:, :], 2.0, None, ALU.mult)
                        v.tensor_scalar(negd[:, :], disc_sb[:, :], -1.0, None, ALU.mult)
                        ve.tick(v.memset(ones_row[:, :], 1.0))

                    emit_ln(blk, "a", xT_r, hidden, [])

            with tmp_scope():
                with nc.named_scope("p12"), nc.Block("p12") as blk:
                    upsl = sb_tmp("upsl", [128, 2, NHC, 128], BF16)
                    gasl = sb_tmp("gasl", [128, 2, NHC, 128], BF16)
                    slr = DmaRing(es, nc, "p12sl", 2)
                    dbuf = sb_tmp("dbuf", [128, 2, 512], F32)
                    labs = sb_tmp("labs", [128, 2, 512], F32)
                    sgb = sb_tmp("sgb", [128, 2, 512], F32)
                    auc = sb_tmp("auc", [128, 2, 512], BF16)
                    ausc = sb_tmp("ausc", [128, 2, 512], BF16)
                    logc = sb_tmp("logc", [128, 2, 512], F32)
                    str_ = DmaRing(es, nc, "p12st", 2)
                    dg = Guard(2); lg = Guard(2); sg_g = Guard(2)
                    bankg = Guard(4)

                    for sc_i in range(NSC):
                        sl = sc_i % 2

                        @blk.gpsimd
                        def _(g, sc_i=sc_i, sl=sl):
                            slr.g.wait(g, sc_i)
                            ssl = slice(sc_i * 128, (sc_i + 1) * 128)
                            sm = slr.sem(sc_i)
                            sm.tick(g.dma_start(upsl[:, sl, 0:16, :], upT_r[:, 0:16, ssl]), 16)
                            sm.tick(g.dma_start(upsl[:, sl, 16:32, :], upT_r[:, 16:32, ssl]), 16)
                            sm.tick(g.dma_start(gasl[:, sl, 0:16, :], gateT_r[:, 0:16, ssl]), 16)
                            sm.tick(g.dma_start(gasl[:, sl, 16:32, :], gateT_r[:, 16:32, ssl]), 16)
                        gwh, gwv = slr.waitv(sc_i)

                        for tci in range(nt):
                            it = sc_i * nt + tci
                            ab = it % 2
                            gb = 2 + it % 2
                            tsl = slice(tci * 512, (tci + 1) * 512)

                            @blk.tensor
                            def _(t, sl=sl, tci=tci, ab=ab, gb=gb, gwh=gwh, gwv=gwv, tsl=tsl):
                                if tci == 0:
                                    t.wait_ge(gwh, gwv)
                                bankg.wait(t, ab)
                                for hc in range(NHC):
                                    i1 = t.matmul(ps[:, ab, :], upsl[:, sl, hc, :], hidden[:, hc, tsl], start=(hc == 0), stop=(hc == NHC - 1))
                                pe.tick(i1)
                                bankg.wait(t, gb)
                                for hc in range(NHC):
                                    i2 = t.matmul(ps[:, gb, :], gasl[:, sl, hc, :], hidden[:, hc, tsl], start=(hc == 0), stop=(hc == NHC - 1))
                                pe.tick(i2)
                            pev_au = pe.n - 1
                            pev_g = pe.n

                            @blk.vector
                            def _(v, it=it, ab=ab, gb=gb, sc_i=sc_i, pev_au=pev_au, pev_g=pev_g):
                                sl2 = it % 2
                                v.wait_ge(pe.h, pev_au)
                                str_.g.wait(v, it)
                                v.tensor_copy(auc[:, sl2, :], ps[:, ab, :])
                                v.wait_ge(pe.h, pev_g)
                                dg.wait(v, it)
                                ve.tick(v.tensor_scalar(dbuf[:, sl2, :], ps[:, gb, :], lp_sb[:, sc_i:sc_i + 1], None, ALU.subtract))
                            vev1 = ve.n
                            bankg.set(gb, ve)

                            @blk.scalar
                            def _(sc, it=it, vev1=vev1):
                                sl2 = it % 2
                                sc.wait_ge(ve.h, vev1)
                                lg.wait(sc, it)
                                sc.activation(labs[:, sl2, :], dbuf[:, sl2, :], AF.Abs)
                                sc.drain()
                                sc.activation(logc[:, sl2, :], labs[:, sl2, :], AF.Ln)
                                sg_g.wait(sc, it)
                                ac.tick(sc.activation(sgb[:, sl2, :], dbuf[:, sl2, :], AF.Sign))
                            acv = ac.n
                            dg.set(it, ac)
                            lg.set(it, ac)

                            @blk.vector
                            def _(v, it=it, ab=ab, acv=acv):
                                sl2 = it % 2
                                v.wait_ge(ac.h, acv)
                                ve.tick(v.tensor_tensor(ausc[:, sl2, :], sgb[:, sl2, :], ps[:, ab, :], ALU.mult))
                            vev2 = ve.n
                            bankg.set(ab, ve)
                            sg_g.set(it, ve)

                            @blk.sync
                            def _(sy, it=it, sc_i=sc_i, tsl=tsl, vev2=vev2, acv=acv):
                                sl2 = it % 2
                                rs = slice(sc_i * 128, (sc_i + 1) * 128)
                                sm = str_.sem(it)
                                sy.wait_ge(ve.h, vev2)
                                sm.tick(sy.dma_start(au_scr[rs, tsl], auc[:, sl2, :]), 16)
                                sm.tick(sy.dma_start(aus_scr[rs, tsl], ausc[:, sl2, :]), 16)
                                sy.wait_ge(ac.h, acv)
                                sm.tick(sy.dma_start(logd_scr[rs, tsl], logc[:, sl2, :]), 16)
                            str_.g.set(it, str_.sem(it))
                        slr.g.set(sc_i, pe)

                    @blk.sync
                    def _(sy, w=[(s.h, s.n) for s in str_.s]):
                        for hh, vv in w:
                            sy.wait_ge(hh, vv)
            p12_waits = [(s.h, s.n) for s in str_.s]

            with tmp_scope():
                with nc.named_scope("p4"), nc.Block("p4") as blk:
                    gsl = sb_tmp("gsl", [128, 2, NHC, 128], BF16)
                    usl = sb_tmp("usl", [128, 2, NHC, 128], BF16)
                    slr4 = DmaRing(es, nc, "p4sl", 2)
                    sgbuf = sb_tmp("sgbuf", [128, 2, 512], F32)
                    sgg = Guard(2)
                    yst = sb_tmp("yst", [128, 2, tc], BF16)
                    ystr = DmaRing(es, nc, "p4y", 2)
                    bank4 = Guard(4)

                    for fc in range(NFC):
                        sl = fc % 2

                        @blk.gpsimd
                        def _(g, fc=fc, sl=sl):
                            slr4.g.wait(g, fc)
                            fsl = slice(fc * 128, (fc + 1) * 128)
                            sm = slr4.sem(fc)
                            sm.tick(g.dma_start(gsl[:, sl, 0:16, :], ffngT_r[:, 0:16, fsl]), 16)
                            sm.tick(g.dma_start(gsl[:, sl, 16:32, :], ffngT_r[:, 16:32, fsl]), 16)
                            sm.tick(g.dma_start(usl[:, sl, 0:16, :], ffnuT_r[:, 0:16, fsl]), 16)
                            sm.tick(g.dma_start(usl[:, sl, 16:32, :], ffnuT_r[:, 16:32, fsl]), 16)
                        gwh, gwv = slr4.waitv(fc)

                        for tci in range(nt):
                            it = fc * nt + tci
                            pgb = it % 2
                            pub = 2 + it % 2
                            tsl = slice(tci * 512, (tci + 1) * 512)

                            @blk.tensor
                            def _(t, sl=sl, tci=tci, pgb=pgb, pub=pub, gwh=gwh, gwv=gwv, tsl=tsl):
                                if tci == 0:
                                    t.wait_ge(gwh, gwv)
                                bank4.wait(t, pgb)
                                for hc in range(NHC):
                                    i1 = t.matmul(ps[:, pgb, :], gsl[:, sl, hc, :], hidden[:, hc, tsl], start=(hc == 0), stop=(hc == NHC - 1))
                                pe.tick(i1)
                                bank4.wait(t, pub)
                                for hc in range(NHC):
                                    i2 = t.matmul(ps[:, pub, :], usl[:, sl, hc, :], hidden[:, hc, tsl], start=(hc == 0), stop=(hc == NHC - 1))
                                pe.tick(i2)
                            pev_g = pe.n - 1
                            pev_u = pe.n

                            @blk.scalar
                            def _(sc, it=it, pgb=pgb, pev_g=pev_g):
                                sl2 = it % 2
                                sc.wait_ge(pe.h, pev_g)
                                sgg.wait(sc, it)
                                ac.tick(sc.activation(sgbuf[:, sl2, :], ps[:, pgb, :], AF.Silu))
                            acv = ac.n
                            bank4.set(pgb, ac)

                            @blk.vector
                            def _(v, it=it, slv=sl, pub=pub, acv=acv, pev_u=pev_u, tsl=tsl, tci=tci, fc=fc):
                                sl2 = it % 2
                                v.wait_ge(ac.h, acv)
                                v.wait_ge(pe.h, pev_u)
                                if tci == 0:
                                    ystr.g.wait(v, fc)
                                ve.tick(v.tensor_tensor(yst[:, slv, tsl], sgbuf[:, sl2, :], ps[:, pub, :], ALU.mult))
                            bank4.set(pub, ve)
                            sgg.set(it, ve)

                        vev_y = ve.n

                        @blk.sync
                        def _(sy, fc=fc, sl=sl, vev_y=vev_y):
                            sy.wait_ge(ve.h, vev_y)
                            ystr.sem(fc).tick(sy.dma_start(y_scr[fc * 128:(fc + 1) * 128, :], yst[:, sl, :]), 16)
                        ystr.g.set(fc, ystr.sem(fc))
                        slr4.g.set(fc, pe)

                    @blk.sync
                    def _(sy, w=[(s.h, s.n) for s in ystr.s]):
                        for hh, vv in w:
                            sy.wait_ge(hh, vv)
            p4_waits = [(s.h, s.n) for s in ystr.s]

        with tmp_scope():
            with nc.named_scope("p3"), nc.Block("p3") as blk:
                tmp = sb_tmp("tmp", [128, NG * NSC, 512], BF16)
                au_sl = sb_tmp("au_sl", [128, 2, NSC, 512], BF16)
                ausl = sb_tmp("ausl", [128, 2, NSC, 512], BF16)
                logsl = sb_tmp("logsl", [128, NSC, 512], F32)
                srcr = DmaRing(es, nc, "p3src", 2)
                lgr = DmaRing(es, nc, "p3lg", 1)
                mskb = sb_tmp("mskb", [128, 512], F32)
                auskb = sb_tmp("auskb", [128, 512], BF16)
                aukb = sb_tmp("aukb", [128, 512], BF16)
                dtb = sb_tmp("dtb", [128, 2, 512], F32)
                dtg = Guard(2)
                flsl = sb_tmp("flsl", [128, 10, NSC, 512], BF16)
                flr = DmaRing(es, nc, "p3fl", 10)
                flst = DmaRing(es, nc, "p3fst", 4)
                flr2 = DmaRing(es, nc, "p3fl2", 10)
                fl_store_cond = {}
                xcb = sb_tmp("xcb", [128, 2, 512], F32)
                xcr = DmaRing(es, nc, "p3x", 2)
                s1c = sb_tmp("s1c", [128, 2, 512], F32)
                s1str = DmaRing(es, nc, "p3s1", 2)
                bank3 = Guard(8)

                mm_end_prev = pe.n
                for tp in range(nt):
                    tsl = slice(tp * 512, (tp + 1) * 512)
                    par = tp % 2

                    @blk.sync
                    def _(sy, tp=tp, tsl=tsl, par=par):
                        if tp == 0:
                            for hh, vv in p12_waits:
                                sy.wait_ge(hh, vv)
                        srcr.g.wait(sy, tp)
                        sm = srcr.sem(tp)
                        sm.tick(sy.dma_start(ausl[:, par, :, :], aus_r[:, :, tsl]), 16)
                        sm.tick(sy.dma_start(au_sl[:, par, :, :], au_r[:, :, tsl]), 16)
                        lgr.g.wait(sy, 0)
                        lgr.sem(0).tick(sy.dma_start(logsl[:, :, :], logd_r[:, :, tsl]), 16)
                    ldh, ldv = srcr.waitv(tp)
                    lgh, lgv = lgr.waitv(0)

                    sc_done = []
                    for sc_i in range(NSC):
                        @blk.vector
                        def _(v, sc_i=sc_i, ldh=ldh, ldv=ldv, lgh=lgh, lgv=lgv, par=par, tp=tp, mm_end_prev=mm_end_prev):
                            if sc_i == 0:
                                v.wait_ge(ldh, ldv)
                                v.wait_ge(lgh, lgv)
                                if tp > 0:
                                    v.wait_ge(pe.h, mm_end_prev)
                            v.drain()
                            v.tensor_scalar(mskb[:, :], logsl[:, sc_i, :], LOG_THD, None, ALU.is_le)
                            v.drain()
                            v.tensor_tensor(auskb[:, :], ausl[:, par, sc_i, :], mskb[:, :], ALU.mult)
                            ve.tick(v.tensor_tensor(aukb[:, :], au_sl[:, par, sc_i, :], mskb[:, :], ALU.mult))
                        vevm = ve.n

                        for j in range(NG):
                            m = j + 1
                            dsl = j % 2

                            @blk.scalar
                            def _(sc, sc_i=sc_i, j=j, m=m, dsl=dsl, lgh=lgh, lgv=lgv, par=par):
                                if j == 0 and sc_i == 0:
                                    sc.wait_ge(lgh, lgv)
                                dtg.wait(sc, j)
                                ac.tick(sc.activation(dtb[:, dsl, :], logsl[:, sc_i, :], AF.Exp, scale=float(m), bias=negd[:, j:j + 1]))
                            acv = ac.n

                            @blk.vector
                            def _(v, sc_i=sc_i, j=j, m=m, dsl=dsl, acv=acv, vevm=vevm, par=par):
                                v.wait_ge(ac.h, acv)
                                if m >= 5:
                                    v.wait_ge(ve.h, vevm)
                                if m in (1, 3):
                                    src2 = ausl[:, par, sc_i, :]
                                elif m in (2, 4):
                                    src2 = au_sl[:, par, sc_i, :]
                                elif m == 5:
                                    src2 = auskb[:, :]
                                else:
                                    src2 = aukb[:, :]
                                ve.tick(v.tensor_tensor(tmp[:, j * NSC + sc_i, :], dtb[:, dsl, :], src2, ALU.mult))
                            dtg.set(j, ve)
                        sc_done.append(ve.n)

                    for hg in range(8):
                        hsl = slice(hg * 512, (hg + 1) * 512)
                        unit_waits = []
                        for j in range(NTERM):
                            u = (hg * NTERM + j) % 10
                            img = hg * NTERM + j

                            if tp == 0:
                                @blk.gpsimd
                                def _(g, u=u, j=j, hsl=hsl):
                                    flr.g.wait(g, u)
                                    flr.sem(u).tick(g.dma_start(flsl[:, u, :, :], flT_r[j, :, :, hsl]), 16)
                                lw_ = flr.waitv(u)
                                unit_waits.append(lw_)
                                if nt > 1:
                                    @blk.scalar
                                    def _(sc, u=u, img=img, lw_=lw_):
                                        sc.wait_ge(lw_[0], lw_[1])
                                        fsm = flst.sem(img)
                                        sc.wait_ge(fsm.h, fsm.n)
                                        fsm.tick(sc.dma_start(fl_bscr[img, :, :, :], flsl[:, u, :, :]), 16)
                                    fl_store_cond[u] = (flst.sem(img), flst.sem(img).n)
                            else:
                                @blk.sync
                                def _(sy, u=u, img=img, tp=tp):
                                    if tp == 1 and img == 0:
                                        for fsm in flst.s:
                                            sy.wait_ge(fsm.h, fsm.n)
                                    flr.g.wait(sy, u)
                                    flr2.sem(u).tick(sy.dma_start(flsl[:, u, :, :], fl_bscr[img, :, :, :]), 16)
                                unit_waits.append(flr2.waitv(u))

                        for hcl in range(4):
                            hc = hg * 4 + hcl
                            bank = hc % 8

                            @blk.sync
                            def _(sy, hc=hc, tsl=tsl):
                                xcr.g.wait(sy, hc)
                                xcr.sem(hc).tick(sy.dma_start(xcb[:, hc % 2, :], xT_r[:, hc, tsl]), 16)
                            xldh, xldv = xcr.waitv(hc)

                            @blk.tensor
                            def _(t, hg=hg, hcl=hcl, bank=bank, unit_waits=unit_waits, sc_done=sc_done, par=par):
                                if hcl == 0:
                                    for hh, vv in unit_waits:
                                        t.wait_ge(hh, vv)
                                bank3.wait(t, bank)
                                first_grp = (hg == 0 and hcl == 0)
                                inst = None
                                for sc_i in range(NSC):
                                    if first_grp:
                                        t.wait_ge(ve.h, sc_done[sc_i])
                                    for j in range(NTERM):
                                        u = (hg * NTERM + j) % 10
                                        rhs = au_sl[:, par, sc_i, :] if j == 6 else tmp[:, j * NSC + sc_i, :]
                                        inst = t.matmul(ps[:, bank, :], flsl[:, u, sc_i, hcl * 128:(hcl + 1) * 128], rhs,
                                                        start=(j == 0 and sc_i == 0), stop=(j == NTERM - 1 and sc_i == NSC - 1))
                                pe.tick(inst)
                            pev = pe.n

                            @blk.vector
                            def _(v, hc=hc, bank=bank, pev=pev, xldh=xldh, xldv=xldv):
                                v.wait_ge(pe.h, pev)
                                v.wait_ge(xldh, xldv)
                                s1str.g.wait(v, hc)
                                ve.tick(v.tensor_tensor(s1c[:, hc % 2, :], xcb[:, hc % 2, :], ps[:, bank, :], ALU.add))
                            xcr.g.set(hc, ve)
                            bank3.set(bank, ve)
                            vv = ve.n

                            @blk.sync
                            def _(sy, hc=hc, tsl=tsl, vv=vv):
                                sy.wait_ge(ve.h, vv)
                                s1str.sem(hc).tick(sy.dma_start(s1_scr[hc * 128:(hc + 1) * 128, tsl], s1c[:, hc % 2, :]), 16)
                            s1str.g.set(hc, s1str.sem(hc))
                        for j in range(NTERM):
                            u9 = (hg * NTERM + j) % 10
                            flr.g.set(hg * NTERM + j, pe)
                            if tp == 0 and u9 in fl_store_cond:
                                sm_, vv_ = fl_store_cond[u9]
                                flr.g.add(hg * NTERM + j, sm_, vv_)
                    mm_end_prev = pe.n
                    srcr.g.set(tp, pe)
                    lgr.g.set(0, ac)

                @blk.sync
                def _(sy, w=[(s.h, s.n) for s in s1str.s]):
                    for hh, vv in w:
                        sy.wait_ge(hh, vv)
            p3_waits = [(s.h, s.n) for s in s1str.s]

        with tmp_scope():
            with nc.named_scope("p5"), nc.Block("p5") as blk:
                ysl = sb_tmp("ysl", [128, NFC, 512], BF16)
                yr5 = DmaRing(es, nc, "p5y", 4)
                dwsl = sb_tmp("dwsl", [128, 5, 39, 256], BF16)
                dwr = DmaRing(es, nc, "p5dw", 5)
                dwst = DmaRing(es, nc, "p5dst", 4)
                dwr2 = DmaRing(es, nc, "p5dw2", 5)
                dw_store_cond = {}
                s1cb = sb_tmp("s1cb", [128, 2, 512], F32)
                s1r5 = DmaRing(es, nc, "p5s1", 2)
                x1c = sb_tmp("x1c", [128, 2, 512], F32)
                x1str = DmaRing(es, nc, "p5x1", 2)
                bank5 = Guard(8)

                for tp in range(nt):
                    tsl = slice(tp * 512, (tp + 1) * 512)

                    ysec = [(0, 20), (20, 40), (40, 60), (60, 78)]

                    @blk.sync
                    def _(sy, tp=tp, tsl=tsl, ysec=ysec):
                        if tp == 0:
                            for hh, vv in p4_waits:
                                sy.wait_ge(hh, vv)
                        for k, (a, b) in enumerate(ysec):
                            yr5.g.wait(sy, k)
                            yr5.sem(k).tick(sy.dma_start(ysl[:, a:b, :], y_r[:, a:b, tsl]), 16)
                    ywaits = [yr5.waitv(k) for k in range(4)]
                    ybound = {a: k for k, (a, b) in enumerate(ysec)}

                    for hg in range(16):
                        hsl = slice(hg * 256, (hg + 1) * 256)
                        unit_waits = []
                        for half in range(2):
                            u = (tp * 16 + hg) * 2 + half
                            img = hg * 2 + half
                            fc0 = half * 39

                            if tp == 0:
                                @blk.gpsimd
                                def _(g, u=u, fc0=fc0, hsl=hsl):
                                    dwr.g.wait(g, u)
                                    sm = dwr.sem(u)
                                    sm.tick(g.dma_start(dwsl[:, u % 5, 0:20, :], downT_r[:, fc0:fc0 + 20, hsl]), 16)
                                    sm.tick(g.dma_start(dwsl[:, u % 5, 20:39, :], downT_r[:, fc0 + 20:fc0 + 39, hsl]), 16)
                                lw_ = dwr.waitv(u)
                                unit_waits.append(lw_)
                                if nt > 1:
                                    @blk.scalar
                                    def _(sc, u=u, img=img, lw_=lw_):
                                        sc.wait_ge(lw_[0], lw_[1])
                                        dsm = dwst.sem(img)
                                        sc.wait_ge(dsm.h, dsm.n)
                                        dsm.tick(sc.dma_start(dw_bscr[img, :, :, :], dwsl[:, u % 5, :, :]), 16)
                                    dw_store_cond[u % 5] = (dwst.sem(img), dwst.sem(img).n)
                            else:
                                if half == 0:
                                    @blk.scalar
                                    def _(sc, u=u, img=img, tp=tp):
                                        if tp == 1 and img == 0:
                                            for dsm in dwst.s:
                                                sc.wait_ge(dsm.h, dsm.n)
                                        dwr.g.wait(sc, u)
                                        dwr2.sem(u).tick(sc.dma_start(dwsl[:, u % 5, :, :], dw_bscr[img, :, :, :]), 16)
                                else:
                                    @blk.sync
                                    def _(sy, u=u, img=img, tp=tp):
                                        if tp == 1 and img == 1:
                                            for dsm in dwst.s:
                                                sy.wait_ge(dsm.h, dsm.n)
                                        dwr.g.wait(sy, u)
                                        dwr2.sem(u).tick(sy.dma_start(dwsl[:, u % 5, :, :], dw_bscr[img, :, :, :]), 16)
                                unit_waits.append(dwr2.waitv(u))

                        for hcl in range(2):
                            hc = hg * 2 + hcl
                            bank = (hg % 4) * 2 + hcl
                            u0 = (tp * 16 + hg) * 2

                            @blk.sync
                            def _(sy, hc=hc, tsl=tsl, tp=tp):
                                if tp == 0 and hc == 0:
                                    for hh, vv in p3_waits:
                                        sy.wait_ge(hh, vv)
                                s1r5.g.wait(sy, hc)
                                s1r5.sem(hc).tick(sy.dma_start(s1cb[:, hc % 2, :], s1_r[:, hc, tsl]), 16)
                            sldh, sldv = s1r5.waitv(hc)

                            @blk.tensor
                            def _(t, hg=hg, hcl=hcl, bank=bank, unit_waits=unit_waits, ywaits=ywaits, ybound=ybound, u0=u0):
                                first_grp = (hg == 0 and hcl == 0)
                                if hcl == 0:
                                    for hh, vv in unit_waits:
                                        t.wait_ge(hh, vv)
                                bank5.wait(t, bank)
                                inst = None
                                for fc in range(NFC):
                                    if first_grp and fc in ybound:
                                        hh, vv = ywaits[ybound[fc]]
                                        t.wait_ge(hh, vv)
                                    half = fc // 39
                                    inst = t.matmul(ps[:, bank, :], dwsl[:, (u0 + half) % 5, fc % 39, hcl * 128:(hcl + 1) * 128], ysl[:, fc, :],
                                                    start=(fc == 0), stop=(fc == NFC - 1))
                                pe.tick(inst)
                            pev = pe.n

                            @blk.vector
                            def _(v, hc=hc, bank=bank, pev=pev, sldh=sldh, sldv=sldv):
                                v.wait_ge(pe.h, pev)
                                v.wait_ge(sldh, sldv)
                                x1str.g.wait(v, hc)
                                ve.tick(v.tensor_tensor(x1c[:, hc % 2, :], s1cb[:, hc % 2, :], ps[:, bank, :], ALU.add))
                            s1r5.g.set(hc, ve)
                            bank5.set(bank, ve)
                            vv = ve.n

                            @blk.sync
                            def _(sy, hc=hc, tsl=tsl, vv=vv):
                                sy.wait_ge(ve.h, vv)
                                x1str.sem(hc).tick(sy.dma_start(x1_scr[hc * 128:(hc + 1) * 128, tsl], x1c[:, hc % 2, :]), 16)
                            x1str.g.set(hc, x1str.sem(hc))
                        for half in range(2):
                            uu = (tp * 16 + hg) * 2 + half
                            dwr.g.set(uu, pe)
                            if tp == 0 and (uu % 5) in dw_store_cond:
                                sm_, vv_ = dw_store_cond[uu % 5]
                                dwr.g.add(uu, sm_, vv_)
                    for k in range(4):
                        yr5.g.set(k, pe)

                @blk.sync
                def _(sy, w=[(s.h, s.n) for s in x1str.s]):
                    for hh, vv in w:
                        sy.wait_ge(hh, vv)
            p5_waits = [(s.h, s.n) for s in x1str.s]

        with tmp_scope():
            hidden1 = sb_tmp("hidden1", [128, NHC, tc], BF16)

            with tmp_scope():
                with nc.named_scope("p6a"), nc.Block("p6a") as blk:
                    emit_ln(blk, "b", x1_r, hidden1, p5_waits)

            with tmp_scope():
                with nc.named_scope("p6b"), nc.Block("p6b") as blk:
                    atsl = sb_tmp("atsl", [128, NHC, 128], BF16)
                    btsl = sb_tmp("btsl", [128, H], BF16)
                    hfl = sb_tmp("hfl", [128, tc], BF16)
                    wsl = sb_tmp("wsl", [128, 2, NHC, 256], BF16)
                    wr6 = DmaRing(es, nc, "p6w", 2)
                    abr = DmaRing(es, nc, "p6ab", 1)
                    outst = sb_tmp("outst", [128, 2, 512], F32)
                    outr = DmaRing(es, nc, "p6o", 2)
                    x1cb = sb_tmp("x1cb", [128, 2, 512], F32)
                    x1r6 = DmaRing(es, nc, "p6x1", 2)
                    bank6 = Guard(4)

                    @blk.gpsimd
                    def _(g):
                        abr.sem(0).tick(g.dma_start(atsl[:, :, :], AT_r[:, :, :]), 16)
                        abr.sem(0).tick(g.dma_start(btsl[:, :], BT[:, :]), 16)
                    abh, abv = abr.waitv(0)

                    for tci in range(nt):
                        tsl = slice(tci * 512, (tci + 1) * 512)
                        bank = tci % 2

                        @blk.tensor
                        def _(t, tci=tci, tsl=tsl, bank=bank, abh=abh, abv=abv):
                            if tci == 0:
                                t.wait_ge(abh, abv)
                            bank6.wait(t, bank)
                            for dc in range(NHC):
                                i1 = t.matmul(ps[:, bank, :], atsl[:, dc, :], hidden1[:, dc, tsl], start=(dc == 0), stop=(dc == NHC - 1))
                            pe.tick(i1)
                        pev = pe.n

                        @blk.vector
                        def _(v, tsl=tsl, bank=bank, pev=pev):
                            v.wait_ge(pe.h, pev)
                            ve.tick(v.tensor_scalar(hfl[:, tsl], ps[:, bank, :], coefcol[:, :], None, ALU.mult))
                        bank6.set(bank, ve)
                    hfl_done = ve.n

                    for og in range(16):
                        sl = og % 2

                        @blk.gpsimd
                        def _(g, og=og, sl=sl):
                            wr6.g.wait(g, og)
                            osl = slice(og * 256, (og + 1) * 256)
                            sm = wr6.sem(og)
                            sm.tick(g.dma_start(wsl[:, sl, 0:16, :], WT_r[:, 0:16, osl]), 16)
                            sm.tick(g.dma_start(wsl[:, sl, 16:32, :], WT_r[:, 16:32, osl]), 16)
                        gwh, gwv = wr6.waitv(og)

                        for ocl in range(2):
                            oc = og * 2 + ocl
                            for tci in range(nt):
                                it = oc * nt + tci
                                bank = it % 4
                                tsl = slice(tci * 512, (tci + 1) * 512)

                                @blk.tensor
                                def _(t, sl=sl, ocl=ocl, oc=oc, tci=tci, tsl=tsl, bank=bank, gwh=gwh, gwv=gwv, hfl_done=hfl_done):
                                    if ocl == 0 and tci == 0:
                                        t.wait_ge(gwh, gwv)
                                        if oc == 0:
                                            t.wait_ge(ve.h, hfl_done)
                                    bank6.wait(t, bank)
                                    for dc in range(NHC):
                                        t.matmul(ps[:, bank, :], wsl[:, sl, dc, ocl * 128:(ocl + 1) * 128], hidden1[:, dc, tsl], start=(dc == 0), stop=False)
                                    inst = t.matmul(ps[:, bank, :], btsl[:, oc * 128:(oc + 1) * 128], hfl[:, tsl], start=False, stop=True)
                                    pe.tick(inst)
                                pev = pe.n

                                @blk.sync
                                def _(sy, oc=oc, tsl=tsl, it=it):
                                    x1r6.g.wait(sy, it)
                                    x1r6.sem(it).tick(sy.dma_start(x1cb[:, it % 2, :], x1_r[:, oc, tsl]), 16)
                                xldh, xldv = x1r6.waitv(it)

                                @blk.vector
                                def _(v, oc=oc, it=it, bank=bank, pev=pev, xldh=xldh, xldv=xldv):
                                    v.wait_ge(pe.h, pev)
                                    v.wait_ge(xldh, xldv)
                                    outr.g.wait(v, it)
                                    ve.tick(v.scalar_tensor_tensor(outst[:, it % 2, :], ps[:, bank, :], bias_sb[:, oc:oc + 1], x1cb[:, it % 2, :], ALU.add, ALU.add))
                                x1r6.g.set(it, ve)
                                bank6.set(bank, ve)
                                vv = ve.n

                                @blk.sync
                                def _(sy, oc=oc, tsl=tsl, it=it, vv=vv):
                                    sy.wait_ge(ve.h, vv)
                                    outr.sem(it).tick(sy.dma_start(outT[oc * 128:(oc + 1) * 128, tsl], outst[:, it % 2, :]), 16)
                                outr.g.set(it, outr.sem(it))
                        wr6.g.set(og, pe)

                    out_waits = [(s.h, s.n) for s in outr.s]

                    @blk.sync
                    def _(sy, out_waits=out_waits):
                        for hh, vv in out_waits:
                            sy.wait_ge(hh, vv)

    return nc


_NC_CACHE = {}


def host_prep(L0_attn_hidden_states, L0_attn_up_w, L0_attn_gate_w, L0_attn_local_point,
              L0_attn_local_approx_output, L0_attn_fuse_weight, L0_attn_discount_factor,
              L0_attn_ffn_gate_w, L0_attn_ffn_up_w, L0_attn_ffn_down_w,
              L0_ffn_lora_A, L0_ffn_lora_B, L0_ffn_W, L0_ffn_bias, L0_ffn_lora_weights):
    f32 = np.float32
    c = np.ascontiguousarray
    x = np.asarray(L0_attn_hidden_states, dtype=f32).reshape(-1, H)
    xT = c(x.T)

    upT = c(np.asarray(L0_attn_up_w, f32).T)
    gateT = c(np.asarray(L0_attn_gate_w, f32).T)
    fuse = np.asarray(L0_attn_fuse_weight, f32)
    lao = np.asarray(L0_attn_local_approx_output, f32)
    flT = c(np.concatenate([fuse.transpose(2, 1, 0),
                            lao.T[None, :, :]], axis=0))
    ffngT = c(np.asarray(L0_attn_ffn_gate_w, f32).T)
    ffnuT = c(np.asarray(L0_attn_ffn_up_w, f32).T)
    downT = c(np.asarray(L0_attn_ffn_down_w, f32).T)
    WT = c(np.asarray(L0_ffn_W, f32).T)
    AT = c(np.asarray(L0_ffn_lora_A, f32).reshape(128, H).T)
    BT = c(np.asarray(L0_ffn_lora_B, f32).transpose(0, 2, 1).reshape(128, H))
    lp_col = c(np.asarray(L0_attn_local_point, f32).reshape(NSC, 128).T)
    bias_col = c(np.asarray(L0_ffn_bias, f32).reshape(NHC, 128).T)
    disc_cols = c(np.broadcast_to(np.asarray(L0_attn_discount_factor, f32), (128, NG)))
    lw_col = c(np.repeat(np.asarray(L0_ffn_lora_weights, f32), 16).reshape(128, 1))

    shared = dict(upT=upT, gateT=gateT, flT=flT, ffngT=ffngT, ffnuT=ffnuT,
                  downT=downT, WT=WT, AT=AT, BT=BT, lp_col=lp_col,
                  bias_col=bias_col, disc_cols=disc_cols, lw_col=lw_col)
    return xT, shared


def kernel(**inputs):
    xT, shared = host_prep(**inputs)
    c = np.ascontiguousarray
    tc = 2048
    if "nc" not in _NC_CACHE:
        _NC_CACHE["nc"] = build(tc=tc)
    nc = _NC_CACHE["nc"]

    in_maps = []
    for ci in range(8):
        m = dict(shared)
        m["xT"] = c(xT[:, ci * tc:(ci + 1) * tc])
        in_maps.append(m)

    res = run_bass_kernel_spmd(nc, in_maps, core_ids=list(range(8)))
    outT = np.concatenate([r["outT"] for r in res.results], axis=1)
    return c(outT.T).reshape(1, 16384, H).astype(np.float32)


# version 10
# speedup vs baseline: 1.0314x; 1.0314x over previous
import contextlib
import numpy as np

import concourse.bass as bass
import concourse.mybir as mybir
from concourse.bass_utils import run_bass_kernel_spmd

F32 = mybir.dt.float32
BF16 = mybir.dt.bfloat16
AF = mybir.ActivationFunctionType
ALU = mybir.AluOpType

H = 4096
S = 1024
FF = 9984
NG = 6
NTERM = 7
NHC = H // 128
NSC = S // 128
NFC = FF // 128
EPS = 1e-5
LOG_THD = float(np.log(np.float32(2.5)))


class Sem:
    def __init__(self, es, nc, name):
        self.h = es.enter_context(nc.semaphore(name))
        self.n = 0

    def tick(self, inst, k=1):
        inst.then_inc(self.h, k)
        self.n += k
        return self.n


class Guard:

    def __init__(self, nslots):
        self.p = [None] * nslots

    def wait(self, eng, i):
        c = self.p[i % len(self.p)]
        if c is not None:
            for sem, val in c:
                eng.wait_ge(sem.h, val)

    def set(self, i, sem, val=None):
        self.p[i % len(self.p)] = [(sem, sem.n if val is None else val)]

    def add(self, i, sem, val=None):
        self.p[i % len(self.p)].append((sem, sem.n if val is None else val))


class DmaRing:

    def __init__(self, es, nc, name, n):
        self.n = n
        self.s = [Sem(es, nc, f"{name}{i}") for i in range(n)]
        self.g = Guard(n)

    def slot(self, i):
        return i % self.n

    def sem(self, i):
        return self.s[i % self.n]

    def waitv(self, i):
        sm = self.s[i % self.n]
        return sm.h, sm.n


def build(tc=2048, dbg=False):
    assert tc % 512 == 0 and tc <= 2048
    nt = tc // 512

    nc = bass.Bass("TRN2", target_bir_lowering=False)

    dt_in = lambda name, shape: nc.dram_tensor(name, shape, F32, kind="ExternalInput").ap()
    xT = dt_in("xT", [H, tc])
    upT = dt_in("upT", [H, S])
    gateT = dt_in("gateT", [H, S])
    flT = dt_in("flT", [NTERM, S, H])
    ffngT = dt_in("ffngT", [H, FF])
    ffnuT = dt_in("ffnuT", [H, FF])
    downT = dt_in("downT", [FF, H])
    WT = dt_in("WT", [H, H])
    AT = dt_in("AT", [H, 128])
    BT = dt_in("BT", [128, H])
    lp_col = dt_in("lp_col", [128, NSC])
    bias_col = dt_in("bias_col", [128, NHC])
    disc_cols = dt_in("disc_cols", [128, NG])
    lw_col = dt_in("lw_col", [128, 1])

    outT = nc.dram_tensor("outT", [H, tc], F32, kind="ExternalOutput").ap()

    skind = "ExternalOutput" if dbg else "Internal"
    au_scr = nc.dram_tensor("au_scr", [S, tc], BF16, kind=skind).ap()
    aus_scr = nc.dram_tensor("aus_scr", [S, tc], BF16, kind=skind).ap()
    logd_scr = nc.dram_tensor("logd_scr", [S, tc], F32, kind=skind).ap()
    y_scr = nc.dram_tensor("y_scr", [FF, tc], BF16, kind=skind).ap()
    s1_scr = nc.dram_tensor("s1_scr", [H, tc], F32, kind=skind).ap()
    x1_scr = nc.dram_tensor("x1_scr", [H, tc], F32, kind=skind).ap()
    fl_bscr = nc.dram_tensor("fl_bscr", [8 * NTERM, 128, NSC, 512], BF16, kind="Internal").ap()
    dw_bscr = nc.dram_tensor("dw_bscr", [32, 128, 39, 256], BF16, kind="Internal").ap()

    xT_r = xT.rearrange("(hc p) t -> p hc t", p=128)
    upT_r = upT.rearrange("(hc p) s -> p hc s", p=128)
    gateT_r = gateT.rearrange("(hc p) s -> p hc s", p=128)
    flT_r = flT.rearrange("g (sc p) h -> g p sc h", p=128)
    ffngT_r = ffngT.rearrange("(hc p) f -> p hc f", p=128)
    ffnuT_r = ffnuT.rearrange("(hc p) f -> p hc f", p=128)
    downT_r = downT.rearrange("(fc p) h -> p fc h", p=128)
    WT_r = WT.rearrange("(dc p) o -> p dc o", p=128)
    AT_r = AT.rearrange("(dc p) r -> p dc r", p=128)
    au_r = au_scr.rearrange("(sc p) t -> p sc t", p=128)
    aus_r = aus_scr.rearrange("(sc p) t -> p sc t", p=128)
    logd_r = logd_scr.rearrange("(sc p) t -> p sc t", p=128)
    y_r = y_scr.rearrange("(fc p) t -> p fc t", p=128)
    x1_r = x1_scr.rearrange("(hc p) t -> p hc t", p=128)
    s1_r = s1_scr.rearrange("(hc p) t -> p hc t", p=128)

    es = contextlib.ExitStack()
    scopes = [es]

    def sb_tmp(name, shape, dt):
        return scopes[-1].enter_context(nc.sbuf_tensor(name, shape, dt))

    @contextlib.contextmanager
    def tmp_scope():
        s = contextlib.ExitStack()
        scopes.append(s)
        try:
            with s:
                yield
        finally:
            scopes.pop()

    with es:
        pe = Sem(es, nc, "pe")
        ve = Sem(es, nc, "ve")
        ac = Sem(es, nc, "ac")
        po = Sem(es, nc, "po")

        ones_bf = sb_tmp("ones_bf", [128, 1], BF16)
        ones_row = sb_tmp("ones_row", [1, 128], F32)
        lp_sb = sb_tmp("lp_sb", [128, NSC], F32)
        bias_sb = sb_tmp("bias_sb", [128, NHC], F32)
        disc_sb = sb_tmp("disc_sb", [128, NG], F32)
        negd = sb_tmp("negd", [128, NG], F32)
        lw_sb = sb_tmp("lw_sb", [128, 1], F32)
        coefcol = sb_tmp("coefcol", [128, 1], F32)

        ps = es.enter_context(nc.psum_tensor("ps", [128, 8, 512], F32))

        def emit_ln(blk, pfx, src_r, hidden_tile, pre_waits):
            st = sb_tmp(pfx + "lnst", [1, 4, 512], F32)
            rows2 = sb_tmp(pfx + "rows2", [1, 2, tc], F32)
            rstd_b = sb_tmp(pfx + "rstd_b", [128, tc], F32)
            mrstd_b = sb_tmp(pfx + "mrstd_b", [128, tc], F32)
            xbuf = sb_tmp(pfx + "lnx", [128, 2, tc], F32)
            xr = DmaRing(es, nc, pfx + "lnxr", 2)
            xqb = sb_tmp(pfx + "lnq", [128, 2, tc], BF16)
            xbb = sb_tmp(pfx + "lnb", [128, 2, tc], BF16)
            qg = Guard(2)
            bg = Guard(2)
            xcw = 512
            t1v = sb_tmp(pfx + "lnt1v", [128, xcw], F32)
            t1p = sb_tmp(pfx + "lnt1p", [128, xcw], F32)

            first = True
            for hc in range(NHC):
                sl = hc % 2
                sl2 = hc % 2

                if hc % 2 == 0:
                    @blk.sync
                    def _(sy, hc=hc, sl=sl, first=first):
                        if first:
                            for hh, vv in pre_waits:
                                sy.wait_ge(hh, vv)
                        xr.g.wait(sy, hc)
                        xr.sem(hc).tick(sy.dma_start(xbuf[:, sl, :], src_r[:, hc, :]), 16)
                else:
                    @blk.scalar
                    def _(sc2, hc=hc, sl=sl, first=first):
                        if first:
                            for hh, vv in pre_waits:
                                sc2.wait_ge(hh, vv)
                        xr.g.wait(sc2, hc)
                        xr.sem(hc).tick(sc2.dma_start(xbuf[:, sl, :], src_r[:, hc, :]), 16)
                ldh, ldv = xr.waitv(hc)
                first = False

                @blk.scalar
                def _(sc, sl=sl, sl2=sl2, ldh=ldh, ldv=ldv, hc=hc):
                    sc.wait_ge(ldh, ldv)
                    qg.wait(sc, hc)
                    ac.tick(sc.activation(xqb[:, sl2, :], xbuf[:, sl, :], AF.Square))
                acv = ac.n

                @blk.vector
                def _(v, sl=sl, sl2=sl2, ldh=ldh, ldv=ldv, hc=hc):
                    v.wait_ge(ldh, ldv)
                    bg.wait(v, hc)
                    ve.tick(v.tensor_copy(xbb[:, sl2, :], xbuf[:, sl, :]))
                vev = ve.n

                @blk.tensor
                def _(t, sl2=sl2, hc=hc, acv=acv, vev=vev):
                    t.wait_ge(ac.h, acv)
                    t.wait_ge(ve.h, vev)
                    st = (hc == 0)
                    sp = (hc == NHC - 1)
                    inst = None
                    for j in range(nt):
                        tsl = slice(j * 512, (j + 1) * 512)
                        inst = t.matmul(ps[0:1, j, :], ones_bf[:, :], xbb[:, sl2, tsl], start=st, stop=sp)
                        inst = t.matmul(ps[0:1, 4 + j, :], ones_bf[:, :], xqb[:, sl2, tsl], start=st, stop=sp)
                    pe.tick(inst)
                xr.g.set(hc, pe)
                qg.set(hc, pe)
                bg.set(hc, pe)
            mm_done = pe.n

            for j in range(nt):
                jsl = slice(j * 512, (j + 1) * 512)

                @blk.vector
                def _(v, j=j, jsl=jsl, mm_done=mm_done):
                    if j == 0:
                        v.wait_ge(pe.h, mm_done)
                    v.drain()
                    v.tensor_copy(st[0:1, 0, :], ps[0:1, j, :])
                    v.tensor_copy(st[0:1, 1, :], ps[0:1, 4 + j, :])
                    v.drain()
                    v.tensor_scalar(st[0:1, 2, :], st[0:1, 0, :], 1.0 / H, None, ALU.mult)
                    v.tensor_scalar(st[0:1, 3, :], st[0:1, 1, :], 1.0 / H, None, ALU.mult)
                    v.drain()
                    v.tensor_tensor(st[0:1, 0, :], st[0:1, 2, :], st[0:1, 2, :], ALU.mult)
                    v.drain()
                    ve.tick(v.tensor_tensor(st[0:1, 1, :], st[0:1, 3, :], st[0:1, 0, :], ALU.subtract))
                vev = ve.n

                @blk.scalar
                def _(sc, j=j, jsl=jsl, vev=vev):
                    sc.wait_ge(ve.h, vev)
                    sc.activation(st[0:1, 3, :], st[0:1, 1, :], AF.Ln, bias=np.float32(EPS))
                    sc.drain()
                    ac.tick(sc.activation(rows2[0:1, 0, jsl], st[0:1, 3, :], AF.Exp, scale=-0.5))
                acv = ac.n

                @blk.vector
                def _(v, j=j, jsl=jsl, acv=acv):
                    v.wait_ge(ac.h, acv)
                    ve.tick(v.tensor_tensor(rows2[0:1, 1, jsl], st[0:1, 2, :], rows2[0:1, 0, jsl], ALU.mult))
            vev = ve.n

            @blk.tensor
            def _(t):
                t.wait_ge(ve.h, vev)
                inst = None
                for j in range(nt):
                    inst = t.matmul(ps[:, j, :], ones_row[:, :], rows2[0:1, 0, j * 512:(j + 1) * 512], start=True, stop=True)
                    inst = t.matmul(ps[:, 4 + j, :], ones_row[:, :], rows2[0:1, 1, j * 512:(j + 1) * 512], start=True, stop=True)
                pe.tick(inst)
            pev = pe.n

            @blk.vector
            def _(v):
                v.wait_ge(pe.h, pev)
                inst = None
                for j in range(nt):
                    inst = v.tensor_copy(rstd_b[:, j * 512:(j + 1) * 512], ps[:, j, :])
                    inst = v.tensor_copy(mrstd_b[:, j * 512:(j + 1) * 512], ps[:, 4 + j, :])
                ve.tick(inst)
            bc_v = ve.n

            for hc in range(NHC):
                i = NHC + hc
                sl = i % 2

                if hc % 2 == 0:
                    @blk.sync
                    def _(sy, i=i, sl=sl, hc=hc):
                        xr.g.wait(sy, i)
                        xr.sem(i).tick(sy.dma_start(xbuf[:, sl, :], src_r[:, hc, :]), 16)
                else:
                    @blk.scalar
                    def _(sc2, i=i, sl=sl, hc=hc):
                        xr.g.wait(sc2, i)
                        xr.sem(i).tick(sc2.dma_start(xbuf[:, sl, :], src_r[:, hc, :]), 16)
                ldh, ldv = xr.waitv(i)

                if hc % 2 == 0:
                    @blk.vector
                    def _(v, sl=sl, hc=hc, ldh=ldh, ldv=ldv):
                        v.wait_ge(ldh, ldv)
                        for q in range(tc // xcw):
                            qs = slice(q * xcw, (q + 1) * xcw)
                            v.drain()
                            v.tensor_tensor(t1v[:, :], xbuf[:, sl, qs], rstd_b[:, qs], ALU.mult)
                            v.drain()
                            ve.tick(v.tensor_tensor(hidden_tile[:, hc, qs], t1v[:, :], mrstd_b[:, qs], ALU.subtract))
                    xr.g.set(i, ve)
                else:
                    @blk.gpsimd
                    def _(g, sl=sl, hc=hc, ldh=ldh, ldv=ldv, bc_v=bc_v):
                        g.wait_ge(ldh, ldv)
                        g.wait_ge(ve.h, bc_v)
                        for q in range(tc // xcw):
                            qs = slice(q * xcw, (q + 1) * xcw)
                            g.drain()
                            g.tensor_tensor(t1p[:, :], xbuf[:, sl, qs], rstd_b[:, qs], ALU.mult)
                            g.drain()
                            po.tick(g.tensor_tensor(hidden_tile[:, hc, qs], t1p[:, :], mrstd_b[:, qs], ALU.subtract))
                    xr.g.set(i, po)

        with tmp_scope():
            hidden = sb_tmp("hidden", [128, NHC, tc], BF16)

            with tmp_scope():
                tabr = DmaRing(es, nc, "tab", 1)
                with nc.named_scope("p0"), nc.Block("p0") as blk:
                    @blk.sync
                    def _(sy):
                        tabr.sem(0).tick(sy.dma_start(lp_sb[:, :], lp_col[:, :]), 16)
                        tabr.sem(0).tick(sy.dma_start(bias_sb[:, :], bias_col[:, :]), 16)
                        tabr.sem(0).tick(sy.dma_start(disc_sb[:, :], disc_cols[:, :]), 16)
                        tabr.sem(0).tick(sy.dma_start(lw_sb[:, :], lw_col[:, :]), 16)
                    tabh, tabv = tabr.waitv(0)

                    @blk.vector
                    def _(v):
                        v.memset(ones_bf[:, :], 1.0)
                        v.wait_ge(tabh, tabv)
                        v.tensor_scalar(coefcol[:, :], lw_sb[:, :], 2.0, None, ALU.mult)
                        v.tensor_scalar(negd[:, :], disc_sb[:, :], -1.0, None, ALU.mult)
                        ve.tick(v.memset(ones_row[:, :], 1.0))

                    emit_ln(blk, "a", xT_r, hidden, [])

            with tmp_scope():
                with nc.named_scope("p12"), nc.Block("p12") as blk:
                    upsl = sb_tmp("upsl", [128, 2, NHC, 128], BF16)
                    gasl = sb_tmp("gasl", [128, 2, NHC, 128], BF16)
                    slr = DmaRing(es, nc, "p12sl", 2)
                    dbuf = sb_tmp("dbuf", [128, 2, 512], F32)
                    labs = sb_tmp("labs", [128, 2, 512], F32)
                    sgb = sb_tmp("sgb", [128, 2, 512], F32)
                    auc = sb_tmp("auc", [128, 2, 512], BF16)
                    ausc = sb_tmp("ausc", [128, 2, 512], BF16)
                    logc = sb_tmp("logc", [128, 2, 512], F32)
                    str_ = DmaRing(es, nc, "p12st", 2)
                    dg = Guard(2); lg = Guard(2); sg_g = Guard(2)
                    bankg = Guard(4)

                    for sc_i in range(NSC):
                        sl = sc_i % 2

                        @blk.gpsimd
                        def _(g, sc_i=sc_i, sl=sl):
                            slr.g.wait(g, sc_i)
                            ssl = slice(sc_i * 128, (sc_i + 1) * 128)
                            sm = slr.sem(sc_i)
                            sm.tick(g.dma_start(upsl[:, sl, 0:16, :], upT_r[:, 0:16, ssl]), 16)
                            sm.tick(g.dma_start(upsl[:, sl, 16:32, :], upT_r[:, 16:32, ssl]), 16)
                            sm.tick(g.dma_start(gasl[:, sl, 0:16, :], gateT_r[:, 0:16, ssl]), 16)
                            sm.tick(g.dma_start(gasl[:, sl, 16:32, :], gateT_r[:, 16:32, ssl]), 16)
                        gwh, gwv = slr.waitv(sc_i)

                        for tci in range(nt):
                            it = sc_i * nt + tci
                            ab = it % 2
                            gb = 2 + it % 2
                            tsl = slice(tci * 512, (tci + 1) * 512)

                            @blk.tensor
                            def _(t, sl=sl, tci=tci, ab=ab, gb=gb, gwh=gwh, gwv=gwv, tsl=tsl):
                                if tci == 0:
                                    t.wait_ge(gwh, gwv)
                                bankg.wait(t, ab)
                                for hc in range(NHC):
                                    i1 = t.matmul(ps[:, ab, :], upsl[:, sl, hc, :], hidden[:, hc, tsl], start=(hc == 0), stop=(hc == NHC - 1))
                                pe.tick(i1)
                                bankg.wait(t, gb)
                                for hc in range(NHC):
                                    i2 = t.matmul(ps[:, gb, :], gasl[:, sl, hc, :], hidden[:, hc, tsl], start=(hc == 0), stop=(hc == NHC - 1))
                                pe.tick(i2)
                            pev_au = pe.n - 1
                            pev_g = pe.n

                            @blk.vector
                            def _(v, it=it, ab=ab, gb=gb, sc_i=sc_i, pev_au=pev_au, pev_g=pev_g):
                                sl2 = it % 2
                                v.wait_ge(pe.h, pev_au)
                                str_.g.wait(v, it)
                                v.tensor_copy(auc[:, sl2, :], ps[:, ab, :])
                                v.wait_ge(pe.h, pev_g)
                                dg.wait(v, it)
                                ve.tick(v.tensor_scalar(dbuf[:, sl2, :], ps[:, gb, :], lp_sb[:, sc_i:sc_i + 1], None, ALU.subtract))
                            vev1 = ve.n
                            bankg.set(gb, ve)

                            @blk.scalar
                            def _(sc, it=it, vev1=vev1):
                                sl2 = it % 2
                                sc.wait_ge(ve.h, vev1)
                                lg.wait(sc, it)
                                sc.activation(labs[:, sl2, :], dbuf[:, sl2, :], AF.Abs)
                                sc.drain()
                                sc.activation(logc[:, sl2, :], labs[:, sl2, :], AF.Ln)
                                sg_g.wait(sc, it)
                                ac.tick(sc.activation(sgb[:, sl2, :], dbuf[:, sl2, :], AF.Sign))
                            acv = ac.n
                            dg.set(it, ac)
                            lg.set(it, ac)

                            @blk.vector
                            def _(v, it=it, ab=ab, acv=acv):
                                sl2 = it % 2
                                v.wait_ge(ac.h, acv)
                                ve.tick(v.tensor_tensor(ausc[:, sl2, :], sgb[:, sl2, :], ps[:, ab, :], ALU.mult))
                            vev2 = ve.n
                            bankg.set(ab, ve)
                            sg_g.set(it, ve)

                            @blk.sync
                            def _(sy, it=it, sc_i=sc_i, tsl=tsl, vev2=vev2, acv=acv):
                                sl2 = it % 2
                                rs = slice(sc_i * 128, (sc_i + 1) * 128)
                                sm = str_.sem(it)
                                sy.wait_ge(ve.h, vev2)
                                sm.tick(sy.dma_start(au_scr[rs, tsl], auc[:, sl2, :]), 16)
                                sm.tick(sy.dma_start(aus_scr[rs, tsl], ausc[:, sl2, :]), 16)
                                sy.wait_ge(ac.h, acv)
                                sm.tick(sy.dma_start(logd_scr[rs, tsl], logc[:, sl2, :]), 16)
                            str_.g.set(it, str_.sem(it))
                        slr.g.set(sc_i, pe)

                    @blk.sync
                    def _(sy, w=[(s.h, s.n) for s in str_.s]):
                        for hh, vv in w:
                            sy.wait_ge(hh, vv)
            p12_waits = [(s.h, s.n) for s in str_.s]

            with tmp_scope():
                with nc.named_scope("p4"), nc.Block("p4") as blk:
                    gsl = sb_tmp("gsl", [128, 2, NHC, 128], BF16)
                    usl = sb_tmp("usl", [128, 2, NHC, 128], BF16)
                    slr4 = DmaRing(es, nc, "p4sl", 2)
                    sgbuf = sb_tmp("sgbuf", [128, 2, 512], F32)
                    sgg = Guard(2)
                    yst = sb_tmp("yst", [128, 2, tc], BF16)
                    ystr = DmaRing(es, nc, "p4y", 2)
                    bank4 = Guard(4)

                    for fc in range(NFC):
                        sl = fc % 2

                        @blk.gpsimd
                        def _(g, fc=fc, sl=sl):
                            slr4.g.wait(g, fc)
                            fsl = slice(fc * 128, (fc + 1) * 128)
                            sm = slr4.sem(fc)
                            sm.tick(g.dma_start(gsl[:, sl, 0:16, :], ffngT_r[:, 0:16, fsl]), 16)
                            sm.tick(g.dma_start(gsl[:, sl, 16:32, :], ffngT_r[:, 16:32, fsl]), 16)
                            sm.tick(g.dma_start(usl[:, sl, 0:16, :], ffnuT_r[:, 0:16, fsl]), 16)
                            sm.tick(g.dma_start(usl[:, sl, 16:32, :], ffnuT_r[:, 16:32, fsl]), 16)
                        gwh, gwv = slr4.waitv(fc)

                        for tci in range(nt):
                            it = fc * nt + tci
                            pgb = it % 2
                            pub = 2 + it % 2
                            tsl = slice(tci * 512, (tci + 1) * 512)

                            @blk.tensor
                            def _(t, sl=sl, tci=tci, pgb=pgb, pub=pub, gwh=gwh, gwv=gwv, tsl=tsl):
                                if tci == 0:
                                    t.wait_ge(gwh, gwv)
                                bank4.wait(t, pgb)
                                for hc in range(NHC):
                                    i1 = t.matmul(ps[:, pgb, :], gsl[:, sl, hc, :], hidden[:, hc, tsl], start=(hc == 0), stop=(hc == NHC - 1))
                                pe.tick(i1)
                                bank4.wait(t, pub)
                                for hc in range(NHC):
                                    i2 = t.matmul(ps[:, pub, :], usl[:, sl, hc, :], hidden[:, hc, tsl], start=(hc == 0), stop=(hc == NHC - 1))
                                pe.tick(i2)
                            pev_g = pe.n - 1
                            pev_u = pe.n

                            @blk.scalar
                            def _(sc, it=it, pgb=pgb, pev_g=pev_g):
                                sl2 = it % 2
                                sc.wait_ge(pe.h, pev_g)
                                sgg.wait(sc, it)
                                ac.tick(sc.activation(sgbuf[:, sl2, :], ps[:, pgb, :], AF.Silu))
                            acv = ac.n
                            bank4.set(pgb, ac)

                            @blk.vector
                            def _(v, it=it, slv=sl, pub=pub, acv=acv, pev_u=pev_u, tsl=tsl, tci=tci, fc=fc):
                                sl2 = it % 2
                                v.wait_ge(ac.h, acv)
                                v.wait_ge(pe.h, pev_u)
                                if tci == 0:
                                    ystr.g.wait(v, fc)
                                ve.tick(v.tensor_tensor(yst[:, slv, tsl], sgbuf[:, sl2, :], ps[:, pub, :], ALU.mult))
                            bank4.set(pub, ve)
                            sgg.set(it, ve)

                        vev_y = ve.n

                        @blk.sync
                        def _(sy, fc=fc, sl=sl, vev_y=vev_y):
                            sy.wait_ge(ve.h, vev_y)
                            ystr.sem(fc).tick(sy.dma_start(y_scr[fc * 128:(fc + 1) * 128, :], yst[:, sl, :]), 16)
                        ystr.g.set(fc, ystr.sem(fc))
                        slr4.g.set(fc, pe)

                    @blk.sync
                    def _(sy, w=[(s.h, s.n) for s in ystr.s]):
                        for hh, vv in w:
                            sy.wait_ge(hh, vv)
            p4_waits = [(s.h, s.n) for s in ystr.s]

        with tmp_scope():
            with nc.named_scope("p3"), nc.Block("p3") as blk:
                tmp = sb_tmp("tmp", [128, NG * NSC, 512], BF16)
                au_sl = sb_tmp("au_sl", [128, 2, NSC, 512], BF16)
                ausl = sb_tmp("ausl", [128, 2, NSC, 512], BF16)
                logsl = sb_tmp("logsl", [128, NSC, 512], F32)
                srcr = DmaRing(es, nc, "p3src", 2)
                lgr = DmaRing(es, nc, "p3lg", 1)
                mskb = sb_tmp("mskb", [128, 512], F32)
                auskb = sb_tmp("auskb", [128, 512], BF16)
                aukb = sb_tmp("aukb", [128, 512], BF16)
                dtb = sb_tmp("dtb", [128, 2, 512], F32)
                dtg = Guard(2)
                flsl = sb_tmp("flsl", [128, 10, NSC, 512], BF16)
                flr = DmaRing(es, nc, "p3fl", 10)
                flst = DmaRing(es, nc, "p3fst", 4)
                flr2 = DmaRing(es, nc, "p3fl2", 10)
                fl_store_cond = {}
                xcb = sb_tmp("xcb", [128, 2, 512], F32)
                xcr = DmaRing(es, nc, "p3x", 2)
                s1c = sb_tmp("s1c", [128, 2, 512], F32)
                s1str = DmaRing(es, nc, "p3s1", 2)
                bank3 = Guard(8)

                mm_end_prev = pe.n
                for tp in range(nt):
                    tsl = slice(tp * 512, (tp + 1) * 512)
                    par = tp % 2

                    @blk.sync
                    def _(sy, tp=tp, tsl=tsl, par=par):
                        if tp == 0:
                            for hh, vv in p12_waits:
                                sy.wait_ge(hh, vv)
                        srcr.g.wait(sy, tp)
                        sm = srcr.sem(tp)
                        sm.tick(sy.dma_start(ausl[:, par, :, :], aus_r[:, :, tsl]), 16)
                        sm.tick(sy.dma_start(au_sl[:, par, :, :], au_r[:, :, tsl]), 16)
                        lgr.g.wait(sy, 0)
                        lgr.sem(0).tick(sy.dma_start(logsl[:, :, :], logd_r[:, :, tsl]), 16)
                    ldh, ldv = srcr.waitv(tp)
                    lgh, lgv = lgr.waitv(0)

                    sc_done = []
                    for sc_i in range(NSC):
                        @blk.vector
                        def _(v, sc_i=sc_i, ldh=ldh, ldv=ldv, lgh=lgh, lgv=lgv, par=par, tp=tp, mm_end_prev=mm_end_prev):
                            if sc_i == 0:
                                v.wait_ge(ldh, ldv)
                                v.wait_ge(lgh, lgv)
                                if tp > 0:
                                    v.wait_ge(pe.h, mm_end_prev)
                            v.drain()
                            v.tensor_scalar(mskb[:, :], logsl[:, sc_i, :], LOG_THD, None, ALU.is_le)
                            v.drain()
                            v.tensor_tensor(auskb[:, :], ausl[:, par, sc_i, :], mskb[:, :], ALU.mult)
                            ve.tick(v.tensor_tensor(aukb[:, :], au_sl[:, par, sc_i, :], mskb[:, :], ALU.mult))
                        vevm = ve.n

                        for j in range(NG):
                            m = j + 1
                            dsl = j % 2

                            @blk.scalar
                            def _(sc, sc_i=sc_i, j=j, m=m, dsl=dsl, lgh=lgh, lgv=lgv, par=par):
                                if j == 0 and sc_i == 0:
                                    sc.wait_ge(lgh, lgv)
                                dtg.wait(sc, j)
                                ac.tick(sc.activation(dtb[:, dsl, :], logsl[:, sc_i, :], AF.Exp, scale=float(m), bias=negd[:, j:j + 1]))
                            acv = ac.n

                            @blk.vector
                            def _(v, sc_i=sc_i, j=j, m=m, dsl=dsl, acv=acv, vevm=vevm, par=par):
                                v.wait_ge(ac.h, acv)
                                if m >= 5:
                                    v.wait_ge(ve.h, vevm)
                                if m in (1, 3):
                                    src2 = ausl[:, par, sc_i, :]
                                elif m in (2, 4):
                                    src2 = au_sl[:, par, sc_i, :]
                                elif m == 5:
                                    src2 = auskb[:, :]
                                else:
                                    src2 = aukb[:, :]
                                ve.tick(v.tensor_tensor(tmp[:, j * NSC + sc_i, :], dtb[:, dsl, :], src2, ALU.mult))
                            dtg.set(j, ve)
                        sc_done.append(ve.n)

                    for hg in range(8):
                        hsl = slice(hg * 512, (hg + 1) * 512)
                        unit_waits = []
                        for j in range(NTERM):
                            u = (hg * NTERM + j) % 10
                            img = hg * NTERM + j

                            if tp == 0:
                                @blk.gpsimd
                                def _(g, u=u, j=j, hsl=hsl):
                                    flr.g.wait(g, u)
                                    flr.sem(u).tick(g.dma_start(flsl[:, u, :, :], flT_r[j, :, :, hsl]), 16)
                                lw_ = flr.waitv(u)
                                unit_waits.append(lw_)
                                if nt > 1:
                                    @blk.scalar
                                    def _(sc, u=u, img=img, lw_=lw_):
                                        sc.wait_ge(lw_[0], lw_[1])
                                        fsm = flst.sem(img)
                                        sc.wait_ge(fsm.h, fsm.n)
                                        fsm.tick(sc.dma_start(fl_bscr[img, :, :, :], flsl[:, u, :, :]), 16)
                                    fl_store_cond[u] = (flst.sem(img), flst.sem(img).n)
                            else:
                                @blk.gpsimd
                                def _(g, u=u, img=img, tp=tp):
                                    if tp == 1 and img == 0:
                                        for fsm in flst.s:
                                            g.wait_ge(fsm.h, fsm.n)
                                    flr.g.wait(g, u)
                                    flr.sem(u).tick(g.dma_start(flsl[:, u, :, :], fl_bscr[img, :, :, :]), 16)
                                unit_waits.append(flr.waitv(u))

                        for hcl in range(4):
                            hc = hg * 4 + hcl
                            bank = hc % 8

                            @blk.sync
                            def _(sy, hc=hc, tsl=tsl):
                                xcr.g.wait(sy, hc)
                                xcr.sem(hc).tick(sy.dma_start(xcb[:, hc % 2, :], xT_r[:, hc, tsl]), 16)
                            xldh, xldv = xcr.waitv(hc)

                            @blk.tensor
                            def _(t, hg=hg, hcl=hcl, bank=bank, unit_waits=unit_waits, sc_done=sc_done, par=par):
                                if hcl == 0:
                                    for hh, vv in unit_waits:
                                        t.wait_ge(hh, vv)
                                bank3.wait(t, bank)
                                first_grp = (hg == 0 and hcl == 0)
                                inst = None
                                for sc_i in range(NSC):
                                    if first_grp:
                                        t.wait_ge(ve.h, sc_done[sc_i])
                                    for j in range(NTERM):
                                        u = (hg * NTERM + j) % 10
                                        rhs = au_sl[:, par, sc_i, :] if j == 6 else tmp[:, j * NSC + sc_i, :]
                                        inst = t.matmul(ps[:, bank, :], flsl[:, u, sc_i, hcl * 128:(hcl + 1) * 128], rhs,
                                                        start=(j == 0 and sc_i == 0), stop=(j == NTERM - 1 and sc_i == NSC - 1))
                                pe.tick(inst)
                            pev = pe.n

                            @blk.vector
                            def _(v, hc=hc, bank=bank, pev=pev, xldh=xldh, xldv=xldv):
                                v.wait_ge(pe.h, pev)
                                v.wait_ge(xldh, xldv)
                                s1str.g.wait(v, hc)
                                ve.tick(v.tensor_tensor(s1c[:, hc % 2, :], xcb[:, hc % 2, :], ps[:, bank, :], ALU.add))
                            xcr.g.set(hc, ve)
                            bank3.set(bank, ve)
                            vv = ve.n

                            @blk.sync
                            def _(sy, hc=hc, tsl=tsl, vv=vv):
                                sy.wait_ge(ve.h, vv)
                                s1str.sem(hc).tick(sy.dma_start(s1_scr[hc * 128:(hc + 1) * 128, tsl], s1c[:, hc % 2, :]), 16)
                            s1str.g.set(hc, s1str.sem(hc))
                        for j in range(NTERM):
                            u9 = (hg * NTERM + j) % 10
                            flr.g.set(hg * NTERM + j, pe)
                            if tp == 0 and u9 in fl_store_cond:
                                sm_, vv_ = fl_store_cond[u9]
                                flr.g.add(hg * NTERM + j, sm_, vv_)
                    mm_end_prev = pe.n
                    srcr.g.set(tp, pe)
                    lgr.g.set(0, ac)

                @blk.sync
                def _(sy, w=[(s.h, s.n) for s in s1str.s]):
                    for hh, vv in w:
                        sy.wait_ge(hh, vv)
            p3_waits = [(s.h, s.n) for s in s1str.s]

        with tmp_scope():
            with nc.named_scope("p5"), nc.Block("p5") as blk:
                ysl = sb_tmp("ysl", [128, NFC, 512], BF16)
                yr5 = DmaRing(es, nc, "p5y", 4)
                dwsl = sb_tmp("dwsl", [128, 5, 39, 256], BF16)
                dwr = DmaRing(es, nc, "p5dw", 5)
                dwst = DmaRing(es, nc, "p5dst", 4)
                dwr2 = DmaRing(es, nc, "p5dw2", 5)
                dw_store_cond = {}
                s1cb = sb_tmp("s1cb", [128, 2, 512], F32)
                s1r5 = DmaRing(es, nc, "p5s1", 2)
                x1c = sb_tmp("x1c", [128, 2, 512], F32)
                x1str = DmaRing(es, nc, "p5x1", 2)
                bank5 = Guard(8)

                for tp in range(nt):
                    tsl = slice(tp * 512, (tp + 1) * 512)

                    ysec = [(0, 20), (20, 40), (40, 60), (60, 78)]

                    @blk.sync
                    def _(sy, tp=tp, tsl=tsl, ysec=ysec):
                        if tp == 0:
                            for hh, vv in p4_waits:
                                sy.wait_ge(hh, vv)
                        for k, (a, b) in enumerate(ysec):
                            yr5.g.wait(sy, k)
                            yr5.sem(k).tick(sy.dma_start(ysl[:, a:b, :], y_r[:, a:b, tsl]), 16)
                    ywaits = [yr5.waitv(k) for k in range(4)]
                    ybound = {a: k for k, (a, b) in enumerate(ysec)}

                    for hg in range(16):
                        hsl = slice(hg * 256, (hg + 1) * 256)
                        unit_waits = []
                        for half in range(2):
                            u = (tp * 16 + hg) * 2 + half
                            img = hg * 2 + half
                            fc0 = half * 39

                            if tp == 0:
                                @blk.gpsimd
                                def _(g, u=u, fc0=fc0, hsl=hsl):
                                    dwr.g.wait(g, u)
                                    sm = dwr.sem(u)
                                    sm.tick(g.dma_start(dwsl[:, u % 5, 0:20, :], downT_r[:, fc0:fc0 + 20, hsl]), 16)
                                    sm.tick(g.dma_start(dwsl[:, u % 5, 20:39, :], downT_r[:, fc0 + 20:fc0 + 39, hsl]), 16)
                                lw_ = dwr.waitv(u)
                                unit_waits.append(lw_)
                                if nt > 1:
                                    @blk.scalar
                                    def _(sc, u=u, img=img, lw_=lw_):
                                        sc.wait_ge(lw_[0], lw_[1])
                                        dsm = dwst.sem(img)
                                        sc.wait_ge(dsm.h, dsm.n)
                                        dsm.tick(sc.dma_start(dw_bscr[img, :, :, :], dwsl[:, u % 5, :, :]), 16)
                                    dw_store_cond[u % 5] = (dwst.sem(img), dwst.sem(img).n)
                            else:
                                @blk.gpsimd
                                def _(g, u=u, img=img, tp=tp):
                                    if tp == 1 and img == 0:
                                        for dsm in dwst.s:
                                            g.wait_ge(dsm.h, dsm.n)
                                    dwr.g.wait(g, u)
                                    dwr.sem(u).tick(g.dma_start(dwsl[:, u % 5, :, :], dw_bscr[img, :, :, :]), 16)
                                unit_waits.append(dwr.waitv(u))

                        for hcl in range(2):
                            hc = hg * 2 + hcl
                            bank = (hg % 4) * 2 + hcl
                            u0 = (tp * 16 + hg) * 2

                            @blk.sync
                            def _(sy, hc=hc, tsl=tsl, tp=tp):
                                if tp == 0 and hc == 0:
                                    for hh, vv in p3_waits:
                                        sy.wait_ge(hh, vv)
                                s1r5.g.wait(sy, hc)
                                s1r5.sem(hc).tick(sy.dma_start(s1cb[:, hc % 2, :], s1_r[:, hc, tsl]), 16)
                            sldh, sldv = s1r5.waitv(hc)

                            @blk.tensor
                            def _(t, hg=hg, hcl=hcl, bank=bank, unit_waits=unit_waits, ywaits=ywaits, ybound=ybound, u0=u0):
                                first_grp = (hg == 0 and hcl == 0)
                                if hcl == 0:
                                    for hh, vv in unit_waits:
                                        t.wait_ge(hh, vv)
                                bank5.wait(t, bank)
                                inst = None
                                for fc in range(NFC):
                                    if first_grp and fc in ybound:
                                        hh, vv = ywaits[ybound[fc]]
                                        t.wait_ge(hh, vv)
                                    half = fc // 39
                                    inst = t.matmul(ps[:, bank, :], dwsl[:, (u0 + half) % 5, fc % 39, hcl * 128:(hcl + 1) * 128], ysl[:, fc, :],
                                                    start=(fc == 0), stop=(fc == NFC - 1))
                                pe.tick(inst)
                            pev = pe.n

                            @blk.vector
                            def _(v, hc=hc, bank=bank, pev=pev, sldh=sldh, sldv=sldv):
                                v.wait_ge(pe.h, pev)
                                v.wait_ge(sldh, sldv)
                                x1str.g.wait(v, hc)
                                ve.tick(v.tensor_tensor(x1c[:, hc % 2, :], s1cb[:, hc % 2, :], ps[:, bank, :], ALU.add))
                            s1r5.g.set(hc, ve)
                            bank5.set(bank, ve)
                            vv = ve.n

                            @blk.sync
                            def _(sy, hc=hc, tsl=tsl, vv=vv):
                                sy.wait_ge(ve.h, vv)
                                x1str.sem(hc).tick(sy.dma_start(x1_scr[hc * 128:(hc + 1) * 128, tsl], x1c[:, hc % 2, :]), 16)
                            x1str.g.set(hc, x1str.sem(hc))
                        for half in range(2):
                            uu = (tp * 16 + hg) * 2 + half
                            dwr.g.set(uu, pe)
                            if tp == 0 and (uu % 5) in dw_store_cond:
                                sm_, vv_ = dw_store_cond[uu % 5]
                                dwr.g.add(uu, sm_, vv_)
                    for k in range(4):
                        yr5.g.set(k, pe)

                @blk.sync
                def _(sy, w=[(s.h, s.n) for s in x1str.s]):
                    for hh, vv in w:
                        sy.wait_ge(hh, vv)
            p5_waits = [(s.h, s.n) for s in x1str.s]

        with tmp_scope():
            hidden1 = sb_tmp("hidden1", [128, NHC, tc], BF16)

            with tmp_scope():
                with nc.named_scope("p6a"), nc.Block("p6a") as blk:
                    emit_ln(blk, "b", x1_r, hidden1, p5_waits)

            with tmp_scope():
                with nc.named_scope("p6b"), nc.Block("p6b") as blk:
                    atsl = sb_tmp("atsl", [128, NHC, 128], BF16)
                    btsl = sb_tmp("btsl", [128, H], BF16)
                    hfl = sb_tmp("hfl", [128, tc], BF16)
                    wsl = sb_tmp("wsl", [128, 2, NHC, 256], BF16)
                    wr6 = DmaRing(es, nc, "p6w", 2)
                    abr = DmaRing(es, nc, "p6ab", 1)
                    outst = sb_tmp("outst", [128, 2, 512], F32)
                    outr = DmaRing(es, nc, "p6o", 2)
                    x1cb = sb_tmp("x1cb", [128, 2, 512], F32)
                    x1r6 = DmaRing(es, nc, "p6x1", 2)
                    bank6 = Guard(4)

                    @blk.gpsimd
                    def _(g):
                        abr.sem(0).tick(g.dma_start(atsl[:, :, :], AT_r[:, :, :]), 16)
                        abr.sem(0).tick(g.dma_start(btsl[:, :], BT[:, :]), 16)
                    abh, abv = abr.waitv(0)

                    for tci in range(nt):
                        tsl = slice(tci * 512, (tci + 1) * 512)
                        bank = tci % 2

                        @blk.tensor
                        def _(t, tci=tci, tsl=tsl, bank=bank, abh=abh, abv=abv):
                            if tci == 0:
                                t.wait_ge(abh, abv)
                            bank6.wait(t, bank)
                            for dc in range(NHC):
                                i1 = t.matmul(ps[:, bank, :], atsl[:, dc, :], hidden1[:, dc, tsl], start=(dc == 0), stop=(dc == NHC - 1))
                            pe.tick(i1)
                        pev = pe.n

                        @blk.vector
                        def _(v, tsl=tsl, bank=bank, pev=pev):
                            v.wait_ge(pe.h, pev)
                            ve.tick(v.tensor_scalar(hfl[:, tsl], ps[:, bank, :], coefcol[:, :], None, ALU.mult))
                        bank6.set(bank, ve)
                    hfl_done = ve.n

                    for og in range(16):
                        sl = og % 2

                        @blk.gpsimd
                        def _(g, og=og, sl=sl):
                            wr6.g.wait(g, og)
                            osl = slice(og * 256, (og + 1) * 256)
                            sm = wr6.sem(og)
                            sm.tick(g.dma_start(wsl[:, sl, 0:16, :], WT_r[:, 0:16, osl]), 16)
                            sm.tick(g.dma_start(wsl[:, sl, 16:32, :], WT_r[:, 16:32, osl]), 16)
                        gwh, gwv = wr6.waitv(og)

                        for ocl in range(2):
                            oc = og * 2 + ocl
                            for tci in range(nt):
                                it = oc * nt + tci
                                bank = it % 4
                                tsl = slice(tci * 512, (tci + 1) * 512)

                                @blk.tensor
                                def _(t, sl=sl, ocl=ocl, oc=oc, tci=tci, tsl=tsl, bank=bank, gwh=gwh, gwv=gwv, hfl_done=hfl_done):
                                    if ocl == 0 and tci == 0:
                                        t.wait_ge(gwh, gwv)
                                        if oc == 0:
                                            t.wait_ge(ve.h, hfl_done)
                                    bank6.wait(t, bank)
                                    for dc in range(NHC):
                                        t.matmul(ps[:, bank, :], wsl[:, sl, dc, ocl * 128:(ocl + 1) * 128], hidden1[:, dc, tsl], start=(dc == 0), stop=False)
                                    inst = t.matmul(ps[:, bank, :], btsl[:, oc * 128:(oc + 1) * 128], hfl[:, tsl], start=False, stop=True)
                                    pe.tick(inst)
                                pev = pe.n

                                @blk.sync
                                def _(sy, oc=oc, tsl=tsl, it=it):
                                    x1r6.g.wait(sy, it)
                                    x1r6.sem(it).tick(sy.dma_start(x1cb[:, it % 2, :], x1_r[:, oc, tsl]), 16)
                                xldh, xldv = x1r6.waitv(it)

                                @blk.vector
                                def _(v, oc=oc, it=it, bank=bank, pev=pev, xldh=xldh, xldv=xldv):
                                    v.wait_ge(pe.h, pev)
                                    v.wait_ge(xldh, xldv)
                                    outr.g.wait(v, it)
                                    ve.tick(v.scalar_tensor_tensor(outst[:, it % 2, :], ps[:, bank, :], bias_sb[:, oc:oc + 1], x1cb[:, it % 2, :], ALU.add, ALU.add))
                                x1r6.g.set(it, ve)
                                bank6.set(bank, ve)
                                vv = ve.n

                                @blk.sync
                                def _(sy, oc=oc, tsl=tsl, it=it, vv=vv):
                                    sy.wait_ge(ve.h, vv)
                                    outr.sem(it).tick(sy.dma_start(outT[oc * 128:(oc + 1) * 128, tsl], outst[:, it % 2, :]), 16)
                                outr.g.set(it, outr.sem(it))
                        wr6.g.set(og, pe)

                    out_waits = [(s.h, s.n) for s in outr.s]

                    @blk.sync
                    def _(sy, out_waits=out_waits):
                        for hh, vv in out_waits:
                            sy.wait_ge(hh, vv)

    return nc


_NC_CACHE = {}


def host_prep(L0_attn_hidden_states, L0_attn_up_w, L0_attn_gate_w, L0_attn_local_point,
              L0_attn_local_approx_output, L0_attn_fuse_weight, L0_attn_discount_factor,
              L0_attn_ffn_gate_w, L0_attn_ffn_up_w, L0_attn_ffn_down_w,
              L0_ffn_lora_A, L0_ffn_lora_B, L0_ffn_W, L0_ffn_bias, L0_ffn_lora_weights):
    f32 = np.float32
    c = np.ascontiguousarray
    x = np.asarray(L0_attn_hidden_states, dtype=f32).reshape(-1, H)
    xT = c(x.T)

    upT = c(np.asarray(L0_attn_up_w, f32).T)
    gateT = c(np.asarray(L0_attn_gate_w, f32).T)
    fuse = np.asarray(L0_attn_fuse_weight, f32)
    lao = np.asarray(L0_attn_local_approx_output, f32)
    flT = c(np.concatenate([fuse.transpose(2, 1, 0),
                            lao.T[None, :, :]], axis=0))
    ffngT = c(np.asarray(L0_attn_ffn_gate_w, f32).T)
    ffnuT = c(np.asarray(L0_attn_ffn_up_w, f32).T)
    downT = c(np.asarray(L0_attn_ffn_down_w, f32).T)
    WT = c(np.asarray(L0_ffn_W, f32).T)
    AT = c(np.asarray(L0_ffn_lora_A, f32).reshape(128, H).T)
    BT = c(np.asarray(L0_ffn_lora_B, f32).transpose(0, 2, 1).reshape(128, H))
    lp_col = c(np.asarray(L0_attn_local_point, f32).reshape(NSC, 128).T)
    bias_col = c(np.asarray(L0_ffn_bias, f32).reshape(NHC, 128).T)
    disc_cols = c(np.broadcast_to(np.asarray(L0_attn_discount_factor, f32), (128, NG)))
    lw_col = c(np.repeat(np.asarray(L0_ffn_lora_weights, f32), 16).reshape(128, 1))

    shared = dict(upT=upT, gateT=gateT, flT=flT, ffngT=ffngT, ffnuT=ffnuT,
                  downT=downT, WT=WT, AT=AT, BT=BT, lp_col=lp_col,
                  bias_col=bias_col, disc_cols=disc_cols, lw_col=lw_col)
    return xT, shared


def kernel(**inputs):
    xT, shared = host_prep(**inputs)
    c = np.ascontiguousarray
    tc = 2048
    if "nc" not in _NC_CACHE:
        _NC_CACHE["nc"] = build(tc=tc)
    nc = _NC_CACHE["nc"]

    in_maps = []
    for ci in range(8):
        m = dict(shared)
        m["xT"] = c(xT[:, ci * tc:(ci + 1) * tc])
        in_maps.append(m)

    res = run_bass_kernel_spmd(nc, in_maps, core_ids=list(range(8)))
    outT = np.concatenate([r["outT"] for r in res.results], axis=1)
    return c(outT.T).reshape(1, 16384, H).astype(np.float32)


# version 16
# speedup vs baseline: 1.0567x; 1.0245x over previous
import contextlib
import numpy as np

import concourse.bass as bass
import concourse.mybir as mybir
from concourse.bass_utils import run_bass_kernel_spmd

F32 = mybir.dt.float32
BF16 = mybir.dt.bfloat16
AF = mybir.ActivationFunctionType
ALU = mybir.AluOpType

H = 4096
S = 1024
FF = 9984
NG = 6
NTERM = 7
NHC = H // 128
NSC = S // 128
NFC = FF // 128
EPS = 1e-5
LOG_THD = float(np.log(np.float32(2.5)))


class Sem:
    def __init__(self, es, nc, name):
        self.h = es.enter_context(nc.semaphore(name))
        self.n = 0

    def tick(self, inst, k=1):
        inst.then_inc(self.h, k)
        self.n += k
        return self.n


class Guard:

    def __init__(self, nslots):
        self.p = [None] * nslots

    def wait(self, eng, i):
        c = self.p[i % len(self.p)]
        if c is not None:
            for sem, val in c:
                eng.wait_ge(sem.h, val)

    def set(self, i, sem, val=None):
        self.p[i % len(self.p)] = [(sem, sem.n if val is None else val)]

    def add(self, i, sem, val=None):
        self.p[i % len(self.p)].append((sem, sem.n if val is None else val))


class DmaRing:

    def __init__(self, es, nc, name, n):
        self.n = n
        self.s = [Sem(es, nc, f"{name}{i}") for i in range(n)]
        self.g = Guard(n)

    def slot(self, i):
        return i % self.n

    def sem(self, i):
        return self.s[i % self.n]

    def waitv(self, i):
        sm = self.s[i % self.n]
        return sm.h, sm.n


class Precast:

    def __init__(self, es, nc, jobs):
        self.jobs = jobs
        self.k = 0
        self.pcl = Sem(es, nc, "pcl")
        self.pcc = Sem(es, nc, "pcc")
        self.pcs = Sem(es, nc, "pcs")
        self.hist = []

    def bind(self, pc_in, pc_out):
        self.pc_in, self.pc_out = pc_in, pc_out

    def emit(self, blk, n=2):
        for _ in range(n):
            if self.k >= len(self.jobs):
                return
            k = self.k
            self.k += 1
            dst, src, a, b = self.jobs[k]
            sl = k % 2
            vin = self.pc_in[:, sl, 0:a, 0:b]
            vout = self.pc_out[:, sl, 0:a, 0:b]
            cv_prev = self.hist[k - 2][1] if k >= 2 else None
            sv_prev = self.hist[k - 2][2] if k >= 2 else None

            @blk.scalar
            def _(sc, vin=vin, src=src, cv_prev=cv_prev):
                if cv_prev is not None:
                    sc.wait_ge(self.pcc.h, cv_prev)
                self.pcl.tick(sc.dma_start(vin, src), 16)
            lv = self.pcl.n

            @blk.vector
            def _(v, vin=vin, vout=vout, lv=lv, sv_prev=sv_prev):
                v.wait_ge(self.pcl.h, lv)
                if sv_prev is not None:
                    v.wait_ge(self.pcs.h, sv_prev)
                self.pcc.tick(v.tensor_copy(vout, vin))
            cv = self.pcc.n

            @blk.sync
            def _(sy, vout=vout, dst=dst, cv=cv):
                sy.wait_ge(self.pcc.h, cv)
                self.pcs.tick(sy.dma_start(dst, vout), 16)
            self.hist.append((lv, cv, self.pcs.n))


def build(tc=2048, dbg=False):
    assert tc % 512 == 0 and tc <= 2048
    nt = tc // 512

    nc = bass.Bass("TRN2", target_bir_lowering=False)

    dt_in = lambda name, shape: nc.dram_tensor(name, shape, F32, kind="ExternalInput").ap()
    xT = dt_in("xT", [H, tc])
    upT = dt_in("upT", [H, S])
    gateT = dt_in("gateT", [H, S])
    flT = dt_in("flT", [NTERM, S, H])
    ffngT = dt_in("ffngT", [H, FF])
    ffnuT = dt_in("ffnuT", [H, FF])
    downT = dt_in("downT", [FF, H])
    WT = dt_in("WT", [H, H])
    AT = dt_in("AT", [H, 128])
    BT = dt_in("BT", [128, H])
    lp_col = dt_in("lp_col", [128, NSC])
    bias_col = dt_in("bias_col", [128, NHC])
    disc_cols = dt_in("disc_cols", [128, NG])
    lw_col = dt_in("lw_col", [128, 1])

    outT = nc.dram_tensor("outT", [H, tc], F32, kind="ExternalOutput").ap()

    skind = "ExternalOutput" if dbg else "Internal"
    au_scr = nc.dram_tensor("au_scr", [S, tc], BF16, kind=skind).ap()
    aus_scr = nc.dram_tensor("aus_scr", [S, tc], BF16, kind=skind).ap()
    logd_scr = nc.dram_tensor("logd_scr", [S, tc], F32, kind=skind).ap()
    y_scr = nc.dram_tensor("y_scr", [FF, tc], BF16, kind=skind).ap()
    s1_scr = nc.dram_tensor("s1_scr", [H, tc], F32, kind=skind).ap()
    x1_scr = nc.dram_tensor("x1_scr", [H, tc], F32, kind=skind).ap()
    fl_bscr = nc.dram_tensor("fl_bscr", [8 * NTERM, 128, NSC, 512], BF16, kind="Internal").ap()
    dw_bscr = nc.dram_tensor("dw_bscr", [32, 128, 39, 256], BF16, kind="Internal").ap()
    w_bscr = nc.dram_tensor("w_bscr", [16, 128, 32, 256], BF16, kind="Internal").ap()

    xT_r = xT.rearrange("(hc p) t -> p hc t", p=128)
    upT_r = upT.rearrange("(hc p) s -> p hc s", p=128)
    gateT_r = gateT.rearrange("(hc p) s -> p hc s", p=128)
    flT_r = flT.rearrange("g (sc p) h -> g p sc h", p=128)
    ffngT_r = ffngT.rearrange("(hc p) f -> p hc f", p=128)
    ffnuT_r = ffnuT.rearrange("(hc p) f -> p hc f", p=128)
    downT_r = downT.rearrange("(fc p) h -> p fc h", p=128)
    WT_r = WT.rearrange("(dc p) o -> p dc o", p=128)
    AT_r = AT.rearrange("(dc p) r -> p dc r", p=128)
    au_r = au_scr.rearrange("(sc p) t -> p sc t", p=128)
    aus_r = aus_scr.rearrange("(sc p) t -> p sc t", p=128)
    logd_r = logd_scr.rearrange("(sc p) t -> p sc t", p=128)
    y_r = y_scr.rearrange("(fc p) t -> p fc t", p=128)
    x1_r = x1_scr.rearrange("(hc p) t -> p hc t", p=128)
    s1_r = s1_scr.rearrange("(hc p) t -> p hc t", p=128)

    es = contextlib.ExitStack()
    scopes = [es]

    def sb_tmp(name, shape, dt):
        return scopes[-1].enter_context(nc.sbuf_tensor(name, shape, dt))

    @contextlib.contextmanager
    def tmp_scope():
        s = contextlib.ExitStack()
        scopes.append(s)
        try:
            with s:
                yield
        finally:
            scopes.pop()

    pcjobs = []
    for img in range(8 * NTERM):
        hg, j = divmod(img, NTERM)
        for s0 in range(0, NSC, 4):
            for h0 in range(0, 512, 256):
                pcjobs.append((fl_bscr[img, :, s0:s0 + 4, h0:h0 + 256],
                               flT_r[j, :, s0:s0 + 4, hg * 512 + h0:hg * 512 + h0 + 256], 4, 256))
    MK_FL = 16 * len(pcjobs)
    dw_rng = [(a, min(a + 4, 39)) for a in range(0, 39, 4)]
    for img in range(32):
        hg, half = divmod(img, 2)
        fc0 = half * 39
        for a, b in dw_rng:
            pcjobs.append((dw_bscr[img, :, a:b, :],
                           downT_r[:, fc0 + a:fc0 + b, hg * 256:(hg + 1) * 256], b - a, 256))
    MK_DW = 16 * len(pcjobs)
    for og in range(16):
        for a in range(0, 32, 4):
            pcjobs.append((w_bscr[og, :, a:a + 4, :],
                           WT_r[:, a:a + 4, og * 256:(og + 1) * 256], 4, 256))
    MK_W = 16 * len(pcjobs)

    with es:
        pe = Sem(es, nc, "pe")
        ve = Sem(es, nc, "ve")
        ac = Sem(es, nc, "ac")
        po = Sem(es, nc, "po")
        pc = Precast(es, nc, pcjobs)
        pcs = pc.pcs

        ones_bf = sb_tmp("ones_bf", [128, 1], BF16)
        ones_row = sb_tmp("ones_row", [1, 128], F32)
        lp_sb = sb_tmp("lp_sb", [128, NSC], F32)
        bias_sb = sb_tmp("bias_sb", [128, NHC], F32)
        disc_sb = sb_tmp("disc_sb", [128, NG], F32)
        negd = sb_tmp("negd", [128, NG], F32)
        lw_sb = sb_tmp("lw_sb", [128, 1], F32)
        coefcol = sb_tmp("coefcol", [128, 1], F32)

        ps = es.enter_context(nc.psum_tensor("ps", [128, 8, 512], F32))

        def emit_ln(blk, pfx, src_r, hidden_tile, pre_waits):
            st = sb_tmp(pfx + "lnst", [1, 4, 512], F32)
            rows2 = sb_tmp(pfx + "rows2", [1, 2, tc], F32)
            rstd_b = sb_tmp(pfx + "rstd_b", [128, tc], F32)
            mrstd_b = sb_tmp(pfx + "mrstd_b", [128, tc], F32)
            xbuf = sb_tmp(pfx + "lnx", [128, 2, tc], F32)
            xr = DmaRing(es, nc, pfx + "lnxr", 2)
            xqb = sb_tmp(pfx + "lnq", [128, 2, tc], BF16)
            xbb = sb_tmp(pfx + "lnb", [128, 2, tc], BF16)
            qg = Guard(2)
            bg = Guard(2)
            xcw = 512
            t1v = sb_tmp(pfx + "lnt1v", [128, xcw], F32)
            t1p = sb_tmp(pfx + "lnt1p", [128, xcw], F32)

            first = True
            for hc in range(NHC):
                sl = hc % 2
                sl2 = hc % 2

                if hc % 2 == 0:
                    @blk.sync
                    def _(sy, hc=hc, sl=sl, first=first):
                        if first:
                            for hh, vv in pre_waits:
                                sy.wait_ge(hh, vv)
                        xr.g.wait(sy, hc)
                        xr.sem(hc).tick(sy.dma_start(xbuf[:, sl, :], src_r[:, hc, :]), 16)
                else:
                    @blk.scalar
                    def _(sc2, hc=hc, sl=sl, first=first):
                        if first:
                            for hh, vv in pre_waits:
                                sc2.wait_ge(hh, vv)
                        xr.g.wait(sc2, hc)
                        xr.sem(hc).tick(sc2.dma_start(xbuf[:, sl, :], src_r[:, hc, :]), 16)
                ldh, ldv = xr.waitv(hc)
                first = False

                @blk.scalar
                def _(sc, sl=sl, sl2=sl2, ldh=ldh, ldv=ldv, hc=hc):
                    sc.wait_ge(ldh, ldv)
                    qg.wait(sc, hc)
                    ac.tick(sc.activation(xqb[:, sl2, :], xbuf[:, sl, :], AF.Square))
                acv = ac.n

                @blk.vector
                def _(v, sl=sl, sl2=sl2, ldh=ldh, ldv=ldv, hc=hc):
                    v.wait_ge(ldh, ldv)
                    bg.wait(v, hc)
                    ve.tick(v.tensor_copy(xbb[:, sl2, :], xbuf[:, sl, :]))
                vev = ve.n

                @blk.tensor
                def _(t, sl2=sl2, hc=hc, acv=acv, vev=vev):
                    t.wait_ge(ac.h, acv)
                    t.wait_ge(ve.h, vev)
                    st = (hc == 0)
                    sp = (hc == NHC - 1)
                    inst = None
                    for j in range(nt):
                        tsl = slice(j * 512, (j + 1) * 512)
                        inst = t.matmul(ps[0:1, j, :], ones_bf[:, :], xbb[:, sl2, tsl], start=st, stop=sp)
                        inst = t.matmul(ps[0:1, 4 + j, :], ones_bf[:, :], xqb[:, sl2, tsl], start=st, stop=sp)
                    pe.tick(inst)
                xr.g.set(hc, pe)
                qg.set(hc, pe)
                bg.set(hc, pe)
            mm_done = pe.n

            for j in range(nt):
                jsl = slice(j * 512, (j + 1) * 512)

                @blk.vector
                def _(v, j=j, jsl=jsl, mm_done=mm_done):
                    if j == 0:
                        v.wait_ge(pe.h, mm_done)
                    v.drain()
                    v.tensor_copy(st[0:1, 0, :], ps[0:1, j, :])
                    v.tensor_copy(st[0:1, 1, :], ps[0:1, 4 + j, :])
                    v.drain()
                    v.tensor_scalar(st[0:1, 2, :], st[0:1, 0, :], 1.0 / H, None, ALU.mult)
                    v.tensor_scalar(st[0:1, 3, :], st[0:1, 1, :], 1.0 / H, None, ALU.mult)
                    v.drain()
                    v.tensor_tensor(st[0:1, 0, :], st[0:1, 2, :], st[0:1, 2, :], ALU.mult)
                    v.drain()
                    ve.tick(v.tensor_tensor(st[0:1, 1, :], st[0:1, 3, :], st[0:1, 0, :], ALU.subtract))
                vev = ve.n

                @blk.scalar
                def _(sc, j=j, jsl=jsl, vev=vev):
                    sc.wait_ge(ve.h, vev)
                    sc.activation(st[0:1, 3, :], st[0:1, 1, :], AF.Ln, bias=np.float32(EPS))
                    sc.drain()
                    ac.tick(sc.activation(rows2[0:1, 0, jsl], st[0:1, 3, :], AF.Exp, scale=-0.5))
                acv = ac.n

                @blk.vector
                def _(v, j=j, jsl=jsl, acv=acv):
                    v.wait_ge(ac.h, acv)
                    ve.tick(v.tensor_tensor(rows2[0:1, 1, jsl], st[0:1, 2, :], rows2[0:1, 0, jsl], ALU.mult))
            vev = ve.n

            @blk.tensor
            def _(t):
                t.wait_ge(ve.h, vev)
                inst = None
                for j in range(nt):
                    inst = t.matmul(ps[:, j, :], ones_row[:, :], rows2[0:1, 0, j * 512:(j + 1) * 512], start=True, stop=True)
                    inst = t.matmul(ps[:, 4 + j, :], ones_row[:, :], rows2[0:1, 1, j * 512:(j + 1) * 512], start=True, stop=True)
                pe.tick(inst)
            pev = pe.n

            @blk.vector
            def _(v):
                v.wait_ge(pe.h, pev)
                inst = None
                for j in range(nt):
                    inst = v.tensor_copy(rstd_b[:, j * 512:(j + 1) * 512], ps[:, j, :])
                    inst = v.tensor_copy(mrstd_b[:, j * 512:(j + 1) * 512], ps[:, 4 + j, :])
                ve.tick(inst)
            bc_v = ve.n

            for hc in range(NHC):
                i = NHC + hc
                sl = i % 2
                use_g = hc % 5 in (1, 3)

                if hc % 2 == 0:
                    @blk.sync
                    def _(sy, i=i, sl=sl, hc=hc):
                        xr.g.wait(sy, i)
                        xr.sem(i).tick(sy.dma_start(xbuf[:, sl, :], src_r[:, hc, :]), 16)
                else:
                    @blk.scalar
                    def _(sc2, i=i, sl=sl, hc=hc):
                        xr.g.wait(sc2, i)
                        xr.sem(i).tick(sc2.dma_start(xbuf[:, sl, :], src_r[:, hc, :]), 16)
                ldh, ldv = xr.waitv(i)

                if not use_g:
                    @blk.vector
                    def _(v, sl=sl, hc=hc, ldh=ldh, ldv=ldv):
                        v.wait_ge(ldh, ldv)
                        for q in range(tc // xcw):
                            qs = slice(q * xcw, (q + 1) * xcw)
                            v.drain()
                            v.tensor_tensor(t1v[:, :], xbuf[:, sl, qs], rstd_b[:, qs], ALU.mult)
                            v.drain()
                            ve.tick(v.tensor_tensor(hidden_tile[:, hc, qs], t1v[:, :], mrstd_b[:, qs], ALU.subtract))
                    xr.g.set(i, ve)
                else:
                    @blk.gpsimd
                    def _(g, sl=sl, hc=hc, ldh=ldh, ldv=ldv, bc_v=bc_v):
                        g.wait_ge(ldh, ldv)
                        g.wait_ge(ve.h, bc_v)
                        for q in range(tc // xcw):
                            qs = slice(q * xcw, (q + 1) * xcw)
                            g.drain()
                            g.tensor_tensor(t1p[:, :], xbuf[:, sl, qs], rstd_b[:, qs], ALU.mult)
                            g.drain()
                            po.tick(g.tensor_tensor(hidden_tile[:, hc, qs], t1p[:, :], mrstd_b[:, qs], ALU.subtract))
                    xr.g.set(i, po)

        with tmp_scope():
            hidden = sb_tmp("hidden", [128, NHC, tc], BF16)

            with tmp_scope():
                tabr = DmaRing(es, nc, "tab", 1)
                with nc.named_scope("p0"), nc.Block("p0") as blk:
                    @blk.sync
                    def _(sy):
                        tabr.sem(0).tick(sy.dma_start(lp_sb[:, :], lp_col[:, :]), 16)
                        tabr.sem(0).tick(sy.dma_start(bias_sb[:, :], bias_col[:, :]), 16)
                        tabr.sem(0).tick(sy.dma_start(disc_sb[:, :], disc_cols[:, :]), 16)
                        tabr.sem(0).tick(sy.dma_start(lw_sb[:, :], lw_col[:, :]), 16)
                    tabh, tabv = tabr.waitv(0)

                    @blk.vector
                    def _(v):
                        v.memset(ones_bf[:, :], 1.0)
                        v.wait_ge(tabh, tabv)
                        v.tensor_scalar(coefcol[:, :], lw_sb[:, :], 2.0, None, ALU.mult)
                        v.tensor_scalar(negd[:, :], disc_sb[:, :], -1.0, None, ALU.mult)
                        ve.tick(v.memset(ones_row[:, :], 1.0))

                    emit_ln(blk, "a", xT_r, hidden, [])

            pc_in = sb_tmp("pc_in", [128, 2, 4, 256], F32)
            pc_out = sb_tmp("pc_out", [128, 2, 4, 256], BF16)
            pc.bind(pc_in, pc_out)

            with tmp_scope():
                with nc.named_scope("p12"), nc.Block("p12") as blk:
                    upsl = sb_tmp("upsl", [128, 2, NHC, 128], BF16)
                    gasl = sb_tmp("gasl", [128, 2, NHC, 128], BF16)
                    slr = DmaRing(es, nc, "p12sl", 2)
                    dbuf = sb_tmp("dbuf", [128, 2, 512], F32)
                    labs = sb_tmp("labs", [128, 2, 512], F32)
                    sgb = sb_tmp("sgb", [128, 2, 512], F32)
                    auc = sb_tmp("auc", [128, 2, 512], BF16)
                    ausc = sb_tmp("ausc", [128, 2, 512], BF16)
                    logc = sb_tmp("logc", [128, 2, 512], F32)
                    str_ = DmaRing(es, nc, "p12st", 2)
                    dg = Guard(2); lg = Guard(2); sg_g = Guard(2)
                    bankg = Guard(4)

                    for sc_i in range(NSC):
                        sl = sc_i % 2

                        @blk.gpsimd
                        def _(g, sc_i=sc_i, sl=sl):
                            slr.g.wait(g, sc_i)
                            ssl = slice(sc_i * 128, (sc_i + 1) * 128)
                            sm = slr.sem(sc_i)
                            sm.tick(g.dma_start(upsl[:, sl, 0:16, :], upT_r[:, 0:16, ssl]), 16)
                            sm.tick(g.dma_start(upsl[:, sl, 16:32, :], upT_r[:, 16:32, ssl]), 16)
                            sm.tick(g.dma_start(gasl[:, sl, 0:16, :], gateT_r[:, 0:16, ssl]), 16)
                            sm.tick(g.dma_start(gasl[:, sl, 16:32, :], gateT_r[:, 16:32, ssl]), 16)
                        gwh, gwv = slr.waitv(sc_i)

                        for tci in range(nt):
                            it = sc_i * nt + tci
                            ab = it % 2
                            gb = 2 + it % 2
                            tsl = slice(tci * 512, (tci + 1) * 512)

                            @blk.tensor
                            def _(t, sl=sl, tci=tci, ab=ab, gb=gb, gwh=gwh, gwv=gwv, tsl=tsl):
                                if tci == 0:
                                    t.wait_ge(gwh, gwv)
                                bankg.wait(t, ab)
                                for hc in range(NHC):
                                    i1 = t.matmul(ps[:, ab, :], upsl[:, sl, hc, :], hidden[:, hc, tsl], start=(hc == 0), stop=(hc == NHC - 1))
                                pe.tick(i1)
                                bankg.wait(t, gb)
                                for hc in range(NHC):
                                    i2 = t.matmul(ps[:, gb, :], gasl[:, sl, hc, :], hidden[:, hc, tsl], start=(hc == 0), stop=(hc == NHC - 1))
                                pe.tick(i2)
                            pev_au = pe.n - 1
                            pev_g = pe.n

                            @blk.vector
                            def _(v, it=it, ab=ab, gb=gb, sc_i=sc_i, pev_au=pev_au, pev_g=pev_g):
                                sl2 = it % 2
                                v.wait_ge(pe.h, pev_au)
                                str_.g.wait(v, it)
                                v.tensor_copy(auc[:, sl2, :], ps[:, ab, :])
                                v.wait_ge(pe.h, pev_g)
                                dg.wait(v, it)
                                ve.tick(v.tensor_scalar(dbuf[:, sl2, :], ps[:, gb, :], lp_sb[:, sc_i:sc_i + 1], None, ALU.subtract))
                            vev1 = ve.n
                            bankg.set(gb, ve)

                            @blk.scalar
                            def _(sc, it=it, vev1=vev1):
                                sl2 = it % 2
                                sc.wait_ge(ve.h, vev1)
                                lg.wait(sc, it)
                                sc.activation(labs[:, sl2, :], dbuf[:, sl2, :], AF.Abs)
                                sc.drain()
                                sc.activation(logc[:, sl2, :], labs[:, sl2, :], AF.Ln)
                                sg_g.wait(sc, it)
                                ac.tick(sc.activation(sgb[:, sl2, :], dbuf[:, sl2, :], AF.Sign))
                            acv = ac.n
                            dg.set(it, ac)
                            lg.set(it, ac)

                            @blk.vector
                            def _(v, it=it, ab=ab, acv=acv):
                                sl2 = it % 2
                                v.wait_ge(ac.h, acv)
                                ve.tick(v.tensor_tensor(ausc[:, sl2, :], sgb[:, sl2, :], ps[:, ab, :], ALU.mult))
                            vev2 = ve.n
                            bankg.set(ab, ve)
                            sg_g.set(it, ve)

                            @blk.sync
                            def _(sy, it=it, sc_i=sc_i, tsl=tsl, vev2=vev2, acv=acv):
                                sl2 = it % 2
                                rs = slice(sc_i * 128, (sc_i + 1) * 128)
                                sm = str_.sem(it)
                                sy.wait_ge(ve.h, vev2)
                                sm.tick(sy.dma_start(au_scr[rs, tsl], auc[:, sl2, :]), 16)
                                sm.tick(sy.dma_start(aus_scr[rs, tsl], ausc[:, sl2, :]), 16)
                                sy.wait_ge(ac.h, acv)
                                sm.tick(sy.dma_start(logd_scr[rs, tsl], logc[:, sl2, :]), 16)
                            str_.g.set(it, str_.sem(it))
                            pc.emit(blk, 2)
                        slr.g.set(sc_i, pe)

                    @blk.sync
                    def _(sy, w=[(s.h, s.n) for s in str_.s]):
                        for hh, vv in w:
                            sy.wait_ge(hh, vv)
            p12_waits = [(s.h, s.n) for s in str_.s]

            with tmp_scope():
                with nc.named_scope("p4"), nc.Block("p4") as blk:
                    gsl = sb_tmp("gsl", [128, 2, NHC, 128], BF16)
                    usl = sb_tmp("usl", [128, 2, NHC, 128], BF16)
                    slr4 = DmaRing(es, nc, "p4sl", 2)
                    sgbuf = sb_tmp("sgbuf", [128, 2, 512], F32)
                    sgg = Guard(2)
                    yst = sb_tmp("yst", [128, 2, tc], BF16)
                    ystr = DmaRing(es, nc, "p4y", 2)
                    bank4 = Guard(4)

                    for fc in range(NFC):
                        sl = fc % 2

                        @blk.gpsimd
                        def _(g, fc=fc, sl=sl):
                            slr4.g.wait(g, fc)
                            fsl = slice(fc * 128, (fc + 1) * 128)
                            sm = slr4.sem(fc)
                            sm.tick(g.dma_start(gsl[:, sl, 0:16, :], ffngT_r[:, 0:16, fsl]), 16)
                            sm.tick(g.dma_start(gsl[:, sl, 16:32, :], ffngT_r[:, 16:32, fsl]), 16)
                            sm.tick(g.dma_start(usl[:, sl, 0:16, :], ffnuT_r[:, 0:16, fsl]), 16)
                            sm.tick(g.dma_start(usl[:, sl, 16:32, :], ffnuT_r[:, 16:32, fsl]), 16)
                        gwh, gwv = slr4.waitv(fc)

                        for tci in range(nt):
                            it = fc * nt + tci
                            pgb = it % 2
                            pub = 2 + it % 2
                            tsl = slice(tci * 512, (tci + 1) * 512)

                            @blk.tensor
                            def _(t, sl=sl, tci=tci, pgb=pgb, pub=pub, gwh=gwh, gwv=gwv, tsl=tsl):
                                if tci == 0:
                                    t.wait_ge(gwh, gwv)
                                bank4.wait(t, pgb)
                                for hc in range(NHC):
                                    i1 = t.matmul(ps[:, pgb, :], gsl[:, sl, hc, :], hidden[:, hc, tsl], start=(hc == 0), stop=(hc == NHC - 1))
                                pe.tick(i1)
                                bank4.wait(t, pub)
                                for hc in range(NHC):
                                    i2 = t.matmul(ps[:, pub, :], usl[:, sl, hc, :], hidden[:, hc, tsl], start=(hc == 0), stop=(hc == NHC - 1))
                                pe.tick(i2)
                            pev_g = pe.n - 1
                            pev_u = pe.n

                            @blk.scalar
                            def _(sc, it=it, pgb=pgb, pev_g=pev_g):
                                sl2 = it % 2
                                sc.wait_ge(pe.h, pev_g)
                                sgg.wait(sc, it)
                                ac.tick(sc.activation(sgbuf[:, sl2, :], ps[:, pgb, :], AF.Silu))
                            acv = ac.n
                            bank4.set(pgb, ac)

                            @blk.vector
                            def _(v, it=it, slv=sl, pub=pub, acv=acv, pev_u=pev_u, tsl=tsl, tci=tci, fc=fc):
                                sl2 = it % 2
                                v.wait_ge(ac.h, acv)
                                v.wait_ge(pe.h, pev_u)
                                if tci == 0:
                                    ystr.g.wait(v, fc)
                                ve.tick(v.tensor_tensor(yst[:, slv, tsl], sgbuf[:, sl2, :], ps[:, pub, :], ALU.mult))
                            bank4.set(pub, ve)
                            sgg.set(it, ve)
                            pc.emit(blk, 2)

                        vev_y = ve.n

                        @blk.sync
                        def _(sy, fc=fc, sl=sl, vev_y=vev_y):
                            sy.wait_ge(ve.h, vev_y)
                            ystr.sem(fc).tick(sy.dma_start(y_scr[fc * 128:(fc + 1) * 128, :], yst[:, sl, :]), 16)
                        ystr.g.set(fc, ystr.sem(fc))
                        slr4.g.set(fc, pe)

                    while pc.k < len(pc.jobs):
                        pc.emit(blk, 8)

                    @blk.sync
                    def _(sy, w=[(s.h, s.n) for s in ystr.s]):
                        for hh, vv in w:
                            sy.wait_ge(hh, vv)
            p4_waits = [(s.h, s.n) for s in ystr.s]

        with tmp_scope():
            with nc.named_scope("p3"), nc.Block("p3") as blk:
                tmp = sb_tmp("tmp", [128, NG * NSC, 512], BF16)
                au_sl = sb_tmp("au_sl", [128, NSC, 512], BF16)
                ausl = sb_tmp("ausl", [128, NSC, 512], BF16)
                logsl = sb_tmp("logsl", [128, NSC, 512], F32)
                srcr = DmaRing(es, nc, "p3src", 1)
                lgr = DmaRing(es, nc, "p3lg", 1)
                mskb = sb_tmp("mskb", [128, 512], F32)
                auskb = sb_tmp("auskb", [128, 512], BF16)
                aukb = sb_tmp("aukb", [128, 512], BF16)
                dtb = sb_tmp("dtb", [128, 2, 512], F32)
                dtg = Guard(2)
                FLR = 13
                flsl = sb_tmp("flsl", [128, FLR, NSC, 512], BF16)
                flr = DmaRing(es, nc, "p3fl", FLR)
                xcb = sb_tmp("xcb", [128, 2, 512], F32)
                xcr = DmaRing(es, nc, "p3x", 2)
                s1c = sb_tmp("s1c", [128, 2, 512], F32)
                s1str = DmaRing(es, nc, "p3s1", 2)
                bank3 = Guard(8)

                mm_end_prev = pe.n
                for tp in range(nt):
                    tsl = slice(tp * 512, (tp + 1) * 512)
                    par = tp % 2

                    @blk.sync
                    def _(sy, tp=tp, tsl=tsl, par=par):
                        if tp == 0:
                            for hh, vv in p12_waits:
                                sy.wait_ge(hh, vv)
                        srcr.g.wait(sy, tp)
                        sm = srcr.sem(tp)
                        sm.tick(sy.dma_start(ausl[:, :, :], aus_r[:, :, tsl]), 16)
                        sm.tick(sy.dma_start(au_sl[:, :, :], au_r[:, :, tsl]), 16)
                    ldh, ldv = srcr.waitv(tp)

                    @blk.scalar
                    def _(sc, tp=tp, tsl=tsl):
                        lgr.g.wait(sc, 0)
                        lgr.sem(0).tick(sc.dma_start(logsl[:, :, :], logd_r[:, :, tsl]), 16)
                    lgh, lgv = lgr.waitv(0)

                    pending = {}

                    def slab_load(img, tp=tp):
                        u = img % FLR
                        if (img % NTERM) % 2 == 0:
                            @blk.gpsimd
                            def _(g, u=u, img=img, tp=tp):
                                if tp == 0 and img == 0:
                                    g.wait_ge(pcs.h, MK_FL)
                                flr.g.wait(g, img)
                                flr.sem(u).tick(g.dma_start(flsl[:, u, :, :], fl_bscr[img, :, :, :]), 16)
                        else:
                            @blk.scalar
                            def _(sc, u=u, img=img, tp=tp):
                                if tp == 0 and img == 1:
                                    sc.wait_ge(pcs.h, MK_FL)
                                flr.g.wait(sc, img)
                                flr.sem(u).tick(sc.dma_start(flsl[:, u, :, :], fl_bscr[img, :, :, :]), 16)
                        pending[img] = flr.waitv(u)

                    for img_b in (0, 2, 4, 6, 1, 3, 5, 7, 9, 8, 11, 10, 12):
                        slab_load(img_b)

                    sc_done = []
                    po_gate = None
                    for sc_i in range(NSC):
                        @blk.vector
                        def _(v, sc_i=sc_i, ldh=ldh, ldv=ldv, lgh=lgh, lgv=lgv, par=par, tp=tp,
                              mm_end_prev=mm_end_prev, po_gate=po_gate):
                            if sc_i == 0:
                                v.wait_ge(ldh, ldv)
                                v.wait_ge(lgh, lgv)
                                if tp > 0:
                                    v.wait_ge(pe.h, mm_end_prev)
                            if po_gate is not None:
                                v.wait_ge(po.h, po_gate)
                            v.drain()
                            v.tensor_scalar(mskb[:, :], logsl[:, sc_i, :], LOG_THD, None, ALU.is_le)
                            v.drain()
                            v.tensor_tensor(auskb[:, :], ausl[:, sc_i, :], mskb[:, :], ALU.mult)
                            ve.tick(v.tensor_tensor(aukb[:, :], au_sl[:, sc_i, :], mskb[:, :], ALU.mult))
                        vevm = ve.n

                        for j in range(NG):
                            m = j + 1
                            dsl = j % 2

                            @blk.scalar
                            def _(sc, sc_i=sc_i, j=j, m=m, dsl=dsl, lgh=lgh, lgv=lgv, par=par):
                                if j == 0 and sc_i == 0:
                                    sc.wait_ge(lgh, lgv)
                                dtg.wait(sc, j)
                                ac.tick(sc.activation(dtb[:, dsl, :], logsl[:, sc_i, :], AF.Exp, scale=float(m), bias=negd[:, j:j + 1]))
                            acv = ac.n

                            if j % 2 == 0:
                                @blk.vector
                                def _(v, sc_i=sc_i, m=m, j=j, dsl=dsl, acv=acv, vevm=vevm, par=par):
                                    v.wait_ge(ac.h, acv)
                                    if m == 5:
                                        v.wait_ge(ve.h, vevm)
                                    src2 = ausl[:, sc_i, :] if m in (1, 3) else auskb[:, :]
                                    ve.tick(v.tensor_tensor(tmp[:, j * NSC + sc_i, :], dtb[:, dsl, :], src2, ALU.mult))
                                dtg.set(j, ve)
                            else:
                                @blk.gpsimd
                                def _(g, sc_i=sc_i, m=m, j=j, dsl=dsl, acv=acv, vevm=vevm, par=par,
                                      tp=tp, mm_end_prev=mm_end_prev, ldh=ldh, ldv=ldv):
                                    if sc_i == 0 and j == 1:
                                        g.wait_ge(ldh, ldv)
                                        if tp > 0:
                                            g.wait_ge(pe.h, mm_end_prev)
                                    g.wait_ge(ac.h, acv)
                                    if m == 6:
                                        g.wait_ge(ve.h, vevm)
                                    src2 = au_sl[:, sc_i, :] if m in (2, 4) else aukb[:, :]
                                    po.tick(g.tensor_tensor(tmp[:, j * NSC + sc_i, :], dtb[:, dsl, :], src2, ALU.mult))
                                dtg.set(j, po)
                                po_gate = po.n
                        sc_done.append([(ve.h, ve.n), (po.h, po.n)])

                    for hg in range(8):
                        hsl = slice(hg * 512, (hg + 1) * 512)
                        unit_waits = []
                        for j in range(NTERM):
                            img = hg * NTERM + j
                            if img not in pending:
                                slab_load(img)
                            unit_waits.append(pending.pop(img))

                        for hcl in range(4):
                            hc = hg * 4 + hcl
                            bank = hc % 8

                            @blk.sync
                            def _(sy, hc=hc, tsl=tsl):
                                xcr.g.wait(sy, hc)
                                xcr.sem(hc).tick(sy.dma_start(xcb[:, hc % 2, :], xT_r[:, hc, tsl]), 16)
                            xldh, xldv = xcr.waitv(hc)

                            @blk.tensor
                            def _(t, hg=hg, hcl=hcl, bank=bank, unit_waits=unit_waits, sc_done=sc_done, par=par):
                                if hcl == 0:
                                    for hh, vv in unit_waits:
                                        t.wait_ge(hh, vv)
                                bank3.wait(t, bank)
                                first_grp = (hg == 0 and hcl == 0)
                                inst = None
                                for sc_i in range(NSC):
                                    if first_grp:
                                        for hh, vv in sc_done[sc_i]:
                                            t.wait_ge(hh, vv)
                                    for j in range(NTERM):
                                        u = (hg * NTERM + j) % FLR
                                        rhs = au_sl[:, sc_i, :] if j == 6 else tmp[:, j * NSC + sc_i, :]
                                        inst = t.matmul(ps[:, bank, :], flsl[:, u, sc_i, hcl * 128:(hcl + 1) * 128], rhs,
                                                        start=(j == 0 and sc_i == 0), stop=(j == NTERM - 1 and sc_i == NSC - 1))
                                pe.tick(inst)
                            pev = pe.n

                            @blk.vector
                            def _(v, hc=hc, bank=bank, pev=pev, xldh=xldh, xldv=xldv):
                                v.wait_ge(pe.h, pev)
                                v.wait_ge(xldh, xldv)
                                s1str.g.wait(v, hc)
                                ve.tick(v.tensor_tensor(s1c[:, hc % 2, :], xcb[:, hc % 2, :], ps[:, bank, :], ALU.add))
                            xcr.g.set(hc, ve)
                            bank3.set(bank, ve)
                            vv = ve.n

                            @blk.sync
                            def _(sy, hc=hc, tsl=tsl, vv=vv):
                                sy.wait_ge(ve.h, vv)
                                s1str.sem(hc).tick(sy.dma_start(s1_scr[hc * 128:(hc + 1) * 128, tsl], s1c[:, hc % 2, :]), 16)
                            s1str.g.set(hc, s1str.sem(hc))
                        for j in range(NTERM):
                            flr.g.set(hg * NTERM + j, pe)
                    mm_end_prev = pe.n
                    srcr.g.set(tp, pe)
                    lgr.g.set(0, ac)

                @blk.sync
                def _(sy, w=[(s.h, s.n) for s in s1str.s]):
                    for hh, vv in w:
                        sy.wait_ge(hh, vv)
            p3_waits = [(s.h, s.n) for s in s1str.s]

        with tmp_scope():
            with nc.named_scope("p5"), nc.Block("p5") as blk:
                ysl = sb_tmp("ysl", [128, NFC, 512], BF16)
                yr5 = DmaRing(es, nc, "p5y", 4)
                dwsl = sb_tmp("dwsl", [128, 6, 39, 256], BF16)
                dwr = DmaRing(es, nc, "p5dw", 6)
                s1cb = sb_tmp("s1cb", [128, 2, 512], F32)
                s1r5 = DmaRing(es, nc, "p5s1", 2)
                x1c = sb_tmp("x1c", [128, 2, 512], F32)
                x1str = DmaRing(es, nc, "p5x1", 2)
                bank5 = Guard(8)

                for tp in range(nt):
                    tsl = slice(tp * 512, (tp + 1) * 512)

                    ysec = [(0, 20), (20, 40), (40, 60), (60, 78)]

                    @blk.sync
                    def _(sy, tp=tp, tsl=tsl, ysec=ysec):
                        if tp == 0:
                            for hh, vv in p4_waits:
                                sy.wait_ge(hh, vv)
                        for k, (a, b) in enumerate(ysec):
                            yr5.g.wait(sy, k)
                            yr5.sem(k).tick(sy.dma_start(ysl[:, a:b, :], y_r[:, a:b, tsl]), 16)
                    ywaits = [yr5.waitv(k) for k in range(4)]
                    ybound = {a: k for k, (a, b) in enumerate(ysec)}

                    for hg in range(16):
                        hsl = slice(hg * 256, (hg + 1) * 256)
                        unit_waits = []
                        for half in range(2):
                            u = (tp * 16 + hg) * 2 + half
                            img = hg * 2 + half

                            if half == 0:
                                @blk.gpsimd
                                def _(g, u=u, img=img, tp=tp, hg=hg):
                                    if tp == 0 and hg == 0:
                                        g.wait_ge(pcs.h, MK_DW)
                                    dwr.g.wait(g, u)
                                    dwr.sem(u).tick(g.dma_start(dwsl[:, u % 6, :, :], dw_bscr[img, :, :, :]), 16)
                            else:
                                @blk.scalar
                                def _(sc, u=u, img=img, tp=tp, hg=hg):
                                    if tp == 0 and hg == 0:
                                        sc.wait_ge(pcs.h, MK_DW)
                                    dwr.g.wait(sc, u)
                                    dwr.sem(u).tick(sc.dma_start(dwsl[:, u % 6, :, :], dw_bscr[img, :, :, :]), 16)
                            unit_waits.append(dwr.waitv(u))

                        for hcl in range(2):
                            hc = hg * 2 + hcl
                            bank = (hg % 4) * 2 + hcl
                            u0 = (tp * 16 + hg) * 2

                            @blk.sync
                            def _(sy, hc=hc, tsl=tsl, tp=tp):
                                if tp == 0 and hc == 0:
                                    for hh, vv in p3_waits:
                                        sy.wait_ge(hh, vv)
                                s1r5.g.wait(sy, hc)
                                s1r5.sem(hc).tick(sy.dma_start(s1cb[:, hc % 2, :], s1_r[:, hc, tsl]), 16)
                            sldh, sldv = s1r5.waitv(hc)

                            @blk.tensor
                            def _(t, hg=hg, hcl=hcl, bank=bank, unit_waits=unit_waits, ywaits=ywaits, ybound=ybound, u0=u0):
                                first_grp = (hg == 0 and hcl == 0)
                                if hcl == 0:
                                    for hh, vv in unit_waits:
                                        t.wait_ge(hh, vv)
                                bank5.wait(t, bank)
                                inst = None
                                for fc in range(NFC):
                                    if first_grp and fc in ybound:
                                        hh, vv = ywaits[ybound[fc]]
                                        t.wait_ge(hh, vv)
                                    half = fc // 39
                                    inst = t.matmul(ps[:, bank, :], dwsl[:, (u0 + half) % 6, fc % 39, hcl * 128:(hcl + 1) * 128], ysl[:, fc, :],
                                                    start=(fc == 0), stop=(fc == NFC - 1))
                                pe.tick(inst)
                            pev = pe.n

                            @blk.vector
                            def _(v, hc=hc, bank=bank, pev=pev, sldh=sldh, sldv=sldv):
                                v.wait_ge(pe.h, pev)
                                v.wait_ge(sldh, sldv)
                                x1str.g.wait(v, hc)
                                ve.tick(v.tensor_tensor(x1c[:, hc % 2, :], s1cb[:, hc % 2, :], ps[:, bank, :], ALU.add))
                            s1r5.g.set(hc, ve)
                            bank5.set(bank, ve)
                            vv = ve.n

                            @blk.sync
                            def _(sy, hc=hc, tsl=tsl, vv=vv):
                                sy.wait_ge(ve.h, vv)
                                x1str.sem(hc).tick(sy.dma_start(x1_scr[hc * 128:(hc + 1) * 128, tsl], x1c[:, hc % 2, :]), 16)
                            x1str.g.set(hc, x1str.sem(hc))
                        for half in range(2):
                            uu = (tp * 16 + hg) * 2 + half
                            dwr.g.set(uu, pe)
                    for k in range(4):
                        yr5.g.set(k, pe)

                @blk.sync
                def _(sy, w=[(s.h, s.n) for s in x1str.s]):
                    for hh, vv in w:
                        sy.wait_ge(hh, vv)
            p5_waits = [(s.h, s.n) for s in x1str.s]

        with tmp_scope():
            hidden1 = sb_tmp("hidden1", [128, NHC, tc], BF16)

            with tmp_scope():
                with nc.named_scope("p6a"), nc.Block("p6a") as blk:
                    emit_ln(blk, "b", x1_r, hidden1, p5_waits)

            with tmp_scope():
                with nc.named_scope("p6b"), nc.Block("p6b") as blk:
                    atsl = sb_tmp("atsl", [128, NHC, 128], BF16)
                    btsl = sb_tmp("btsl", [128, H], BF16)
                    hfl = sb_tmp("hfl", [128, tc], BF16)
                    wsl = sb_tmp("wsl", [128, 2, NHC, 256], BF16)
                    wr6 = DmaRing(es, nc, "p6w", 2)
                    abr = DmaRing(es, nc, "p6ab", 1)
                    outst = sb_tmp("outst", [128, 2, 512], F32)
                    outr = DmaRing(es, nc, "p6o", 2)
                    x1cb = sb_tmp("x1cb", [128, 2, 512], F32)
                    x1r6 = DmaRing(es, nc, "p6x1", 2)
                    bank6 = Guard(4)

                    @blk.gpsimd
                    def _(g):
                        abr.sem(0).tick(g.dma_start(atsl[:, :, :], AT_r[:, :, :]), 16)
                        abr.sem(0).tick(g.dma_start(btsl[:, :], BT[:, :]), 16)
                    abh, abv = abr.waitv(0)

                    for tci in range(nt):
                        tsl = slice(tci * 512, (tci + 1) * 512)
                        bank = tci % 2

                        @blk.tensor
                        def _(t, tci=tci, tsl=tsl, bank=bank, abh=abh, abv=abv):
                            if tci == 0:
                                t.wait_ge(abh, abv)
                            bank6.wait(t, bank)
                            for dc in range(NHC):
                                i1 = t.matmul(ps[:, bank, :], atsl[:, dc, :], hidden1[:, dc, tsl], start=(dc == 0), stop=(dc == NHC - 1))
                            pe.tick(i1)
                        pev = pe.n

                        @blk.vector
                        def _(v, tsl=tsl, bank=bank, pev=pev):
                            v.wait_ge(pe.h, pev)
                            ve.tick(v.tensor_scalar(hfl[:, tsl], ps[:, bank, :], coefcol[:, :], None, ALU.mult))
                        bank6.set(bank, ve)
                    hfl_done = ve.n

                    for og in range(16):
                        sl = og % 2

                        @blk.gpsimd
                        def _(g, og=og, sl=sl):
                            if og == 0:
                                g.wait_ge(pcs.h, MK_W)
                            wr6.g.wait(g, og)
                            wr6.sem(og).tick(g.dma_start(wsl[:, sl, :, :], w_bscr[og, :, :, :]), 16)
                        gwh, gwv = wr6.waitv(og)

                        for ocl in range(2):
                            oc = og * 2 + ocl
                            for tci in range(nt):
                                it = oc * nt + tci
                                bank = it % 4
                                tsl = slice(tci * 512, (tci + 1) * 512)

                                @blk.tensor
                                def _(t, sl=sl, ocl=ocl, oc=oc, tci=tci, tsl=tsl, bank=bank, gwh=gwh, gwv=gwv, hfl_done=hfl_done):
                                    if ocl == 0 and tci == 0:
                                        t.wait_ge(gwh, gwv)
                                        if oc == 0:
                                            t.wait_ge(ve.h, hfl_done)
                                    bank6.wait(t, bank)
                                    for dc in range(NHC):
                                        t.matmul(ps[:, bank, :], wsl[:, sl, dc, ocl * 128:(ocl + 1) * 128], hidden1[:, dc, tsl], start=(dc == 0), stop=False)
                                    inst = t.matmul(ps[:, bank, :], btsl[:, oc * 128:(oc + 1) * 128], hfl[:, tsl], start=False, stop=True)
                                    pe.tick(inst)
                                pev = pe.n

                                @blk.sync
                                def _(sy, oc=oc, tsl=tsl, it=it):
                                    x1r6.g.wait(sy, it)
                                    x1r6.sem(it).tick(sy.dma_start(x1cb[:, it % 2, :], x1_r[:, oc, tsl]), 16)
                                xldh, xldv = x1r6.waitv(it)

                                @blk.vector
                                def _(v, oc=oc, it=it, bank=bank, pev=pev, xldh=xldh, xldv=xldv):
                                    v.wait_ge(pe.h, pev)
                                    v.wait_ge(xldh, xldv)
                                    outr.g.wait(v, it)
                                    ve.tick(v.scalar_tensor_tensor(outst[:, it % 2, :], ps[:, bank, :], bias_sb[:, oc:oc + 1], x1cb[:, it % 2, :], ALU.add, ALU.add))
                                x1r6.g.set(it, ve)
                                bank6.set(bank, ve)
                                vv = ve.n

                                @blk.sync
                                def _(sy, oc=oc, tsl=tsl, it=it, vv=vv):
                                    sy.wait_ge(ve.h, vv)
                                    outr.sem(it).tick(sy.dma_start(outT[oc * 128:(oc + 1) * 128, tsl], outst[:, it % 2, :]), 16)
                                outr.g.set(it, outr.sem(it))
                        wr6.g.set(og, pe)

                    out_waits = [(s.h, s.n) for s in outr.s]

                    @blk.sync
                    def _(sy, out_waits=out_waits):
                        for hh, vv in out_waits:
                            sy.wait_ge(hh, vv)

    return nc


_NC_CACHE = {}


def host_prep(L0_attn_hidden_states, L0_attn_up_w, L0_attn_gate_w, L0_attn_local_point,
              L0_attn_local_approx_output, L0_attn_fuse_weight, L0_attn_discount_factor,
              L0_attn_ffn_gate_w, L0_attn_ffn_up_w, L0_attn_ffn_down_w,
              L0_ffn_lora_A, L0_ffn_lora_B, L0_ffn_W, L0_ffn_bias, L0_ffn_lora_weights):
    f32 = np.float32
    c = np.ascontiguousarray
    x = np.asarray(L0_attn_hidden_states, dtype=f32).reshape(-1, H)
    xT = c(x.T)

    upT = c(np.asarray(L0_attn_up_w, f32).T)
    gateT = c(np.asarray(L0_attn_gate_w, f32).T)
    fuse = np.asarray(L0_attn_fuse_weight, f32)
    lao = np.asarray(L0_attn_local_approx_output, f32)
    flT = c(np.concatenate([fuse.transpose(2, 1, 0),
                            lao.T[None, :, :]], axis=0))
    ffngT = c(np.asarray(L0_attn_ffn_gate_w, f32).T)
    ffnuT = c(np.asarray(L0_attn_ffn_up_w, f32).T)
    downT = c(np.asarray(L0_attn_ffn_down_w, f32).T)
    WT = c(np.asarray(L0_ffn_W, f32).T)
    AT = c(np.asarray(L0_ffn_lora_A, f32).reshape(128, H).T)
    BT = c(np.asarray(L0_ffn_lora_B, f32).transpose(0, 2, 1).reshape(128, H))
    lp_col = c(np.asarray(L0_attn_local_point, f32).reshape(NSC, 128).T)
    bias_col = c(np.asarray(L0_ffn_bias, f32).reshape(NHC, 128).T)
    disc_cols = c(np.broadcast_to(np.asarray(L0_attn_discount_factor, f32), (128, NG)))
    lw_col = c(np.repeat(np.asarray(L0_ffn_lora_weights, f32), 16).reshape(128, 1))

    shared = dict(upT=upT, gateT=gateT, flT=flT, ffngT=ffngT, ffnuT=ffnuT,
                  downT=downT, WT=WT, AT=AT, BT=BT, lp_col=lp_col,
                  bias_col=bias_col, disc_cols=disc_cols, lw_col=lw_col)
    return xT, shared


def kernel(**inputs):
    xT, shared = host_prep(**inputs)
    c = np.ascontiguousarray
    tc = 2048
    if "nc" not in _NC_CACHE:
        _NC_CACHE["nc"] = build(tc=tc)
    nc = _NC_CACHE["nc"]

    in_maps = []
    for ci in range(8):
        m = dict(shared)
        m["xT"] = c(xT[:, ci * tc:(ci + 1) * tc])
        in_maps.append(m)

    res = run_bass_kernel_spmd(nc, in_maps, core_ids=list(range(8)))
    outT = np.concatenate([r["outT"] for r in res.results], axis=1)
    return c(outT.T).reshape(1, 16384, H).astype(np.float32)
